# version 16
# baseline (speedup 1.0000x reference)
import math
import numpy as np
import ml_dtypes
import concourse.bass as bass
import concourse.mybir as mybir
from concourse.bass_utils import run_bass_kernel_spmd
from contextlib import ExitStack

F32 = mybir.dt.float32
BF16 = mybir.dt.bfloat16
I8 = mybir.dt.int8
AF = mybir.ActivationFunctionType
ALU = mybir.AluOpType
AX = mybir.AxisListType

NCORES = 8
B, T, D = 16, 2048, 2048
TOK = (B // NCORES) * T
STRIPE = 1024
NSTRIPE = TOK // STRIPE
NQT = STRIPE // 128
NH, NKV, HD = 16, 4, 128
DK = D // 128
SCALE = 1.0 / math.sqrt(HD)
EPS = 1e-5
INV127 = float(np.float32(1.0 / 127.0))
INVD = 1.0 / 2048.0
KCOLS = 1280


class Sem:
    def __init__(self, stk, nc, name):
        self.h = stk.enter_context(nc.semaphore(name))
        self.n = 0


class Prog:
    ENGS = ("sync", "tensor", "vector", "scalar", "gpsimd")

    def __init__(self):
        self.q = {e: [] for e in self.ENGS}
        self.last_wait = {e: {} for e in self.ENGS}

    def emit(self, eng, fn, waits=(), inc=None, inck=1):
        lw = self.last_wait[eng]
        rw = []
        for sem, val in waits:
            if sem is None or val is None or val <= 0:
                continue
            if lw.get(id(sem), -1) >= val:
                continue
            lw[id(sem)] = val
            rw.append((sem, val))

        def do(h, fn=fn, rw=tuple(rw), inc=inc, inck=inck):
            for s, v in rw:
                h.wait_ge(s.h, v)
            r = fn(h)
            if inc is not None:
                r.then_inc(inc.h, inck)
        self.q[eng].append(do)
        if inc is not None:
            inc.n += inck
            return inc.n
        return None


def build_nc(nstripe=NSTRIPE, debug=False, maxphase=9):
    nc = bass.Bass("TRN2", target_bir_lowering=False)

    x = nc.dram_tensor("x", [TOK, D], F32, kind="ExternalInput").ap()
    wqT = nc.dram_tensor("wqT", [D, D], F32, kind="ExternalInput").ap()
    wkT = nc.dram_tensor("wkT", [D, 512], F32, kind="ExternalInput").ap()
    wvT = nc.dram_tensor("wvT", [D, 512], F32, kind="ExternalInput").ap()
    woT = nc.dram_tensor("woT", [D, D], F32, kind="ExternalInput").ap()
    wfO = nc.dram_tensor("wfO", [D, D], F32, kind="ExternalInput").ap()
    biasb = nc.dram_tensor("biasb", [128, D], F32, kind="ExternalInput").ap()
    idf = nc.dram_tensor("idf", [128, 128], F32, kind="ExternalInput").ap()
    idb = nc.dram_tensor("idb", [128, 128], BF16, kind="ExternalInput").ap()
    masksrc = nc.dram_tensor("masks", [128, 1160], BF16, kind="ExternalInput").ap()
    y = nc.dram_tensor("y", [TOK, D], F32, kind="ExternalOutput").ap()
    xp = nc.dram_tensor("xp", [TOK, D], F32, kind="Internal").ap()
    wqbf = nc.dram_tensor("wqbf", [D, D], BF16, kind="Internal").ap()
    wkbf = nc.dram_tensor("wkbf", [D, 512], BF16, kind="Internal").ap()
    wvbf = nc.dram_tensor("wvbf", [D, 512], BF16, kind="Internal").ap()
    wobf = nc.dram_tensor("wobf", [D, D], BF16, kind="Internal").ap()
    wfbf = nc.dram_tensor("wfbf", [D, D], BF16, kind="Internal").ap()
    wscl = nc.dram_tensor("wscl", [128, D], F32, kind="Internal").ap()
    if debug:
        dxp = nc.dram_tensor("dxp", [TOK, D], F32, kind="ExternalOutput").ap()
        dkf = nc.dram_tensor("dkf", [128, NKV * KCOLS], BF16, kind="ExternalOutput").ap()
        dvt = nc.dram_tensor("dvt", [128, NKV * KCOLS], BF16, kind="ExternalOutput").ap()
        dq = nc.dram_tensor("dq", [128, 16384], BF16, kind="ExternalOutput").ap()
        dm = nc.dram_tensor("dm", [128, 16384], BF16, kind="ExternalOutput").ap()
        dwf = nc.dram_tensor("dwf", [D, D], BF16, kind="ExternalOutput").ap()
        dws = nc.dram_tensor("dws", [128, D], F32, kind="ExternalOutput").ap()
        dstat = nc.dram_tensor("dstat", [128, 16], F32, kind="ExternalOutput").ap()
        dxs0 = nc.dram_tensor("dxs0", [128, D], F32, kind="ExternalOutput").ap()
        dxs1 = nc.dram_tensor("dxs1", [128, D], F32, kind="ExternalOutput").ap()
        dxnt1 = nc.dram_tensor("dxnt1", [128, D], BF16, kind="ExternalOutput").ap()
        dwk = nc.dram_tensor("dwk", [D, 512], BF16, kind="ExternalOutput").ap()
        dwo = nc.dram_tensor("dwo", [D, D], BF16, kind="ExternalOutput").ap()

    with ExitStack() as stk:
        def S(*a, **k):
            return stk.enter_context(nc.sbuf_tensor(*a, **k)).ap()

        arena_w = S("arena_w", [128, 16 * 2048], BF16)
        arena_q = S("arena_q", [128, 16 * 1024], BF16)
        arena_m = S("arena_m", [128, 16 * 1024], BF16)
        am32 = arena_m.bitcast(F32)
        kf = S("kf", [128, NKV * KCOLS], BF16)
        vt = S("vt", [128, NKV * KCOLS], BF16)
        ksink = S("ksink", [128, 16], BF16)
        vsink = S("vsink", [128, NKV * 128], BF16)
        xstage = [S(f"xstage{i}", [128, D], F32) for i in range(2)]
        xnt = [S(f"xnt{i}", [128, D], BF16) for i in range(2)]
        wstage = [S(f"wstage{i}", [128, D], F32) for i in range(2)]
        xstage.append(wstage[0])
        sqs = S("sqs", [128, 512], F32)
        exb = [S(f"exb{i}", [128, 388], BF16) for i in range(4)]
        pmb = [S(f"pmb{i}", [128, 388], BF16) for i in range(4)]
        diag = [S(f"diag{i}", [128, 128], BF16) for i in range(4)]
        negm = S("negm", [128, 4], F32)
        den = S("den", [128, 4], F32)
        alphab = S("alphab", [128, 4], BF16)
        statc = S("statc", [128, 24], F32)
        epsc = S("epsc", [128, 1], F32)
        jnk = S("jnk", [128, 1], F32)
        onesr = S("onesr", [1, 128], F32)
        maskb = S("maskb", [128, 1160], BF16)
        idbuf = S("idbuf", [128, 128], BF16)
        idfb = S("idfb", [128, 128], F32)
        wmaxt = S("wmaxt", [128, 16], F32)
        wrec = S("wrec", [128, 16], F32)
        wsclT = S("wsclT", [128, 16], F32)
        w8st = S("w8st", [128, D], I8)
        xq8 = w8st
        tmpb = None

        def xn_f(dd):
            return arena_m[:, dd * 1024:(dd + 1) * 1024]
        am3 = arena_m.rearrange("p (c n) -> p c n", n=1024)
        ptbuf = [arena_m[:, 0:2048], arena_m[:, 2048:4096]]
        ring = [arena_m[:, 4096:6144], arena_m[:, 6144:8192]]
        xps = [am32[:, 4096:6144], am32[:, 6144:8192]]
        wsb = am32[:, 0:2048]
        bib = am32[:, 2048:4096]
        outst = am32[:, 4096:6144]
        xqtb = [arena_m[:, 12288:14336], arena_m[:, 14336:16384]]

        wsrow = am32[0:1, 6144:8192]
        tmpv = [xnt[sl // 2].bitcast(F32)[:, (sl % 2) * 512:(sl % 2 + 1) * 512] for sl in range(4)]
        pbank = [stk.enter_context(nc.psum_tensor(f"ps{i}", [128, 512], F32)).ap() for i in range(8)]

        dL = Sem(stk, nc, "dL")
        dO = Sem(stk, nc, "dO")
        dG = Sem(stk, nc, "dG")
        tE = Sem(stk, nc, "tE")
        vE = Sem(stk, nc, "vE")
        sE = Sem(stk, nc, "sE")
        gE = Sem(stk, nc, "gE")
        p = Prog()

        guard = {}

        def G(key):
            g = guard.get(key)
            return [g] if g else []

        def setg(key, sem, val):
            guard[key] = (sem, val)

        BK = [f"bank{i}" for i in range(8)]

        for dst, src in [(idfb, idf), (idbuf, idb), (maskb, masksrc)]:
            p.emit("sync", lambda h, d=dst, s=src: h.dma_start(out=d, in_=s), inc=dL, inck=16)
        c_consts = dL.n
        p.emit("vector", lambda h: h.memset(epsc, EPS))
        p.emit("vector", lambda h: h.memset(onesr, 1.0))
        for i in range(4):
            p.emit("vector", lambda h, i=i: h.memset(diag[i], 0.0))

        prepv = {}

        def emit_ffn_prep():
            kst = [arena_m[:, 4096:4608], arena_m[:, 4608:5120]]
            wfb3 = wfbf.rearrange("(c p) n -> p c n", p=128)
            last_sp = 0
            for t in range(16):
                w = p.emit("sync", lambda h, t=t: h.dma_start(out=wstage[t % 2], in_=wfO[t * 128:(t + 1) * 128, :]),
                           waits=G(f"wst{t % 2}"), inc=dL, inck=16)
                vm = p.emit("vector", lambda h, t=t: h.reduce_max(out=wmaxt[:, t:t + 1], in_=wstage[t % 2],
                                                                  axis=AX.X, apply_absolute_value=True),
                           waits=[(dL, w)], inc=vE)
                p.emit("vector", lambda h: h.tensor_copy(out=jnk, in_=epsc))
                p.emit("vector", lambda h, t=t: h.tensor_scalar(
                    out=wsclT[:, t:t + 1], in0=wmaxt[:, t:t + 1], scalar1=1e-8, scalar2=INV127,
                    op0=ALU.max, op1=ALU.mult), inc=vE)
                p.emit("vector", lambda h: h.tensor_copy(out=jnk, in_=epsc))
                def _wrec(h, t=t):
                    return h.reciprocal(out=wrec[:, t:t + 1], in_=wsclT[:, t:t + 1])
                vrec = p.emit("vector", _wrec, inc=vE)
                s1 = p.emit("scalar", lambda h, t=t: h.activation(out=w8st, in_=wstage[t % 2], func=AF.Identity,
                                                                  scale=wrec[:, t:t + 1]),
                            waits=[(vE, vrec)] + G("w8st"), inc=sE)
                setg(f"wst{t % 2}", sE, s1)
                s2 = p.emit("scalar", lambda h: h.activation(out=xnt[0], in_=w8st, func=AF.Identity, scale=1.0),
                            waits=G("xnt0w"), inc=sE)
                setg("w8st", sE, s2)
                for half in range(4):
                    bi = 6 + half % 2
                    bank = pbank[bi]
                    tw = None
                    for j in range(4):
                        dd = half * 4 + j
                        tw = p.emit("tensor", lambda h, dd=dd, j=j, bank=bank:
                                    h.matmul(out=bank[:, j * 128:(j + 1) * 128],
                                             lhsT=xnt[0][:, dd * 128:(dd + 1) * 128], rhs=idbuf,
                                             start=True, stop=True),
                                    waits=([(sE, s2), (dL, c_consts)] + G(BK[bi])) if j == 0 else [],
                                    inc=tE if j == 3 else None)
                    ev = p.emit("vector", lambda h, half=half, bank=bank:
                                h.tensor_copy(out=kst[half % 2].rearrange("p (c n) -> p c n", n=128),
                                              in_=bank),
                                waits=[(tE, tw)] + G(f"kst{half % 2}") + G("prepm"), inc=vE)
                    setg(BK[bi], vE, ev)
                    sp = p.emit("sync", lambda h, t=t, half=half: h.dma_start(
                        out=wfb3[:, half * 4:(half + 1) * 4, t * 128:(t + 1) * 128],
                        in_=kst[half % 2].rearrange("p (c n) -> p c n", n=128)),
                                waits=[(vE, ev)], inc=dO, inck=16)
                    setg(f"kst{half % 2}", dO, sp)
                    last_sp = sp
                setg("xnt0w", vE, vE.n)
                m1 = p.emit("tensor", lambda h, t=t: h.matmul(out=pbank[6][0:1, 0:128], lhsT=wsclT[:, t:t + 1],
                                                              rhs=idfb, start=True, stop=True),
                            waits=[(vE, vrec), (dL, c_consts)] + G(BK[6]), inc=tE)
                e1 = p.emit("vector", lambda h, t=t: h.tensor_copy(out=wsrow[0:1, t * 128:(t + 1) * 128],
                                                                   in_=pbank[6][0:1, 0:128]),
                            waits=[(tE, m1)], inc=vE)
                setg(BK[6], vE, e1)
                m2 = p.emit("tensor", lambda h, t=t: h.matmul(out=pbank[7][:, 0:128], lhsT=onesr,
                                                              rhs=wsrow[0:1, t * 128:(t + 1) * 128],
                                                              start=True, stop=True),
                            waits=[(vE, e1)] + G(BK[7]), inc=tE)
                e2 = p.emit("vector", lambda h, t=t: h.tensor_copy(out=am32[:, t * 128:(t + 1) * 128],
                                                                   in_=pbank[7][:, 0:128]),
                            waits=[(tE, m2)] + G("prepm"), inc=vE)
                setg(BK[7], vE, e2)
            prepv['wf_spill_done'] = last_sp
            wscl_spill = p.emit("sync", lambda h: h.dma_start(out=wscl, in_=am32[:, 0:2048]),
                                waits=[(vE, vE.n)], inc=dO, inck=16)
            prepv['wscl_spill'] = wscl_spill
            setg("wsrow_rd", dO, wscl_spill)

        spill_record = {}

        for s in range(nstripe):
            tok0 = s * STRIPE
            first = (s == 0)
            seq_start = (s % 2 == 0)

            xload = [None] * NQT

            def emit_xload(qt, src_t, extra=(), nb=2):
                idx = qt % nb
                xload[qt] = p.emit("gpsimd", lambda h, qt=qt, src_t=src_t, tok0=tok0, idx=idx: h.dma_start(
                    out=xstage[idx], in_=src_t[tok0 + qt * 128: tok0 + (qt + 1) * 128, :]),
                           waits=G(f"xst{idx}") + list(extra), inc=dG, inck=16)
                guard.pop(f"xst{idx}", None)

            emit_xload(0, x)
            emit_xload(1, x)

            def load_weight(dst_fn, n_tiles, width, fsrc, bfscr, tag, extra=()):
                done = []
                if first:
                    spills = []
                    for t in range(n_tiles):
                        w = p.emit("sync", lambda h, t=t, width=width, fsrc=fsrc: h.dma_start(
                            out=wstage[t % 2][:, 0:width], in_=fsrc[t * 128:(t + 1) * 128, :]),
                                   waits=G(f"wst{t % 2}"), inc=dL, inck=16)
                        if t % 2 == 0:
                            c = p.emit("vector", lambda h, t=t, width=width: h.tensor_copy(
                                out=dst_fn(t), in_=wstage[t % 2][:, 0:width]),
                                       waits=[(dL, w)] + G(f"{tag}{t}") + list(extra), inc=vE)
                            csem = vE
                        else:
                            c = p.emit("scalar", lambda h, t=t, width=width: h.activation(
                                out=dst_fn(t), in_=wstage[t % 2][:, 0:width], func=AF.Identity, scale=1.0),
                                       waits=[(dL, w)] + G(f"{tag}{t}") + list(extra), inc=sE)
                            csem = sE
                        setg(f"wst{t % 2}", csem, c)
                        sp = p.emit("sync", lambda h, t=t, bfscr=bfscr: h.dma_start(
                            out=bfscr[t * 128:(t + 1) * 128, :], in_=dst_fn(t)),
                                    waits=[(csem, c)], inc=dO, inck=16)
                        spills.append(sp)
                        done.append((csem, c))
                    spill_record[tag] = spills[-1]
                else:
                    for t in range(n_tiles):
                        w = p.emit("sync", lambda h, t=t, bfscr=bfscr: h.dma_start(
                            out=dst_fn(t), in_=bfscr[t * 128:(t + 1) * 128, :]),
                                   waits=G(f"{tag}{t}") + [(dO, spill_record[tag])] + list(extra),
                                   inc=dL, inck=16)
                        done.append((dL, w))
                return done

            wk_tile = lambda t: arena_q[:, t * 512:(t + 1) * 512]
            wv_tile = lambda t: arena_q[:, 8192 + t * 512: 8192 + (t + 1) * 512]
            w_tile = lambda t: arena_w[:, t * 2048:(t + 1) * 2048]

            ln1_done = [None] * NQT
            ln1_s3 = {}

            def ln1_front(qt):
                c0 = qt % 2
                base = c0 * 4
                w = xload[qt]
                v1 = p.emit("vector", lambda h, qt=qt, base=base: h.reduce_sum(
                    out=statc[:, base:base + 1], in_=xstage[qt % 2], axis=AX.X),
                            waits=[(dG, w)] + G(f"mcol{base}"), inc=vE)
                sq = None
                for j in range(4):
                    sq = p.emit("scalar", lambda h, j=j, qt=qt: h.activation(
                        out=sqs, in_=xstage[qt % 2][:, j * 512:(j + 1) * 512], func=AF.Square, scale=1.0,
                        accum_out=statc[:, 8 + j:9 + j]),
                        waits=[(dG, w)] + (G("sqacc") if j == 0 else []), inc=sE)
                v2 = p.emit("vector", lambda h, base=base: h.reduce_sum(
                    out=statc[:, base + 1:base + 2], in_=statc[:, 8:12], axis=AX.X),
                            waits=[(sE, sq)] + G(f"var{c0}"), inc=vE)
                setg("sqacc", vE, v2)
                v3 = p.emit("vector", lambda h, base=base: h.tensor_scalar_mul(
                    out=statc[:, base:base + 1], in0=statc[:, base:base + 1], scalar1=INVD), inc=vE)
                v4 = p.emit("vector", lambda h, base=base: h.tensor_scalar_mul(
                    out=statc[:, base + 1:base + 2], in0=statc[:, base + 1:base + 2], scalar1=INVD), inc=vE)
                v5 = p.emit("vector", lambda h, base=base: h.tensor_tensor(
                    out=statc[:, 12:13], in0=statc[:, base:base + 1], in1=statc[:, base:base + 1],
                    op=ALU.mult),
                            waits=G("col12") + G("col12r"), inc=vE)
                p.emit("vector", lambda h: h.tensor_copy(out=jnk, in_=epsc))
                v6 = p.emit("vector", lambda h, base=base: h.scalar_tensor_tensor(
                    out=statc[:, base + 1:base + 2], in0=statc[:, 12:13], scalar=-1.0,
                    in1=statc[:, base + 1:base + 2], op0=ALU.mult, op1=ALU.add), inc=vE)
                s1 = p.emit("scalar", lambda h, base=base: h.activation(
                    out=statc[:, 12:13], in_=statc[:, base + 1:base + 2], func=AF.Ln, bias=epsc, scale=1.0),
                            waits=[(vE, v6)], inc=sE)
                setg(f"var{c0}", sE, s1)
                setg("col12", sE, s1)
                s2 = p.emit("scalar", lambda h, base=base: h.activation(
                    out=statc[:, base + 2:base + 3], in_=statc[:, 12:13], func=AF.Exp, scale=-0.5),
                            waits=G(f"rs{c0}"), inc=sE)
                setg("col12r", sE, s2)
                v7 = p.emit("vector", lambda h, base=base: h.scalar_tensor_tensor(
                    out=statc[:, base + 3:base + 4], in0=statc[:, base:base + 1], scalar=-1.0,
                    in1=statc[:, base + 2:base + 3], op0=ALU.mult, op1=ALU.mult),
                            waits=[(sE, s2)] + G(f"nb{c0}"), inc=vE)
                s3 = p.emit("scalar", lambda h, qt=qt, base=base: h.activation(
                    out=xnt[qt % 2], in_=xstage[qt % 2], func=AF.Identity,
                    scale=statc[:, base + 2:base + 3], bias=statc[:, base + 3:base + 4]),
                            waits=[(vE, v7)] + G(f"xnt{qt % 2}"), inc=sE)
                setg(f"rs{c0}", sE, s3)
                setg(f"nb{c0}", sE, s3)
                setg(f"mcol{base}", sE, s3)
                setg(f"xst{qt % 2}", sE, s3)
                if qt + 2 < NQT:
                    emit_xload(qt + 2, x)
                ln1_s3[qt] = s3

            def ln1_back(qt):
                s3 = ln1_s3.pop(qt)
                lastev = None
                for half in range(4):
                    bi = 4 + half % 2
                    bank = pbank[bi]
                    tw = None
                    for j in range(4):
                        dd = half * 4 + j
                        tw = p.emit("tensor", lambda h, dd=dd, j=j, bank=bank, qt=qt:
                                    h.matmul(out=bank[:, j * 128:(j + 1) * 128],
                                             lhsT=xnt[qt % 2][:, dd * 128:(dd + 1) * 128], rhs=idbuf,
                                             start=True, stop=True),
                                    waits=([(sE, s3), (dL, c_consts)] + G(BK[bi])) if j == 0 else [],
                                    inc=tE if j == 3 else None)
                    ev = p.emit("vector", lambda h, half=half, bank=bank, qt=qt:
                                h.tensor_copy(out=am3[:, half * 4:(half + 1) * 4, qt * 128:(qt + 1) * 128],
                                              in_=bank),
                                waits=[(tE, tw)] + (G("misc_ln") if (qt == 0 and half == 0) else []), inc=vE)
                    setg(BK[bi], vE, ev)
                    lastev = ev
                setg(f"xnt{qt % 2}", vE, lastev)
                ln1_done[qt] = lastev

            ln1_front(0)
            for qt in range(NQT):
                if qt + 1 < NQT:
                    ln1_front(qt + 1)
                ln1_back(qt)
            xnf_all = max(ln1_done)

            wk_done = load_weight(wk_tile, 16, 512, wkT, wkbf, "tk")
            wv_done = load_weight(wv_tile, 16, 512, wvT, wvbf, "tv")
            wq_done = load_weight(w_tile, 16, 2048, wqT, wqbf, "tq")

            if maxphase < 1:
                continue
            tail_done = 0
            if not seq_start:
                for c in range(NKV):
                    p.emit("vector", lambda h, c=c: h.tensor_copy(
                        out=kf[:, c * KCOLS: c * KCOLS + 256], in_=kf[:, c * KCOLS + 1024: c * KCOLS + 1280]),
                           waits=G("kf_rw"), inc=vE)
                    tail_done = p.emit("vector", lambda h, c=c: h.tensor_copy(
                        out=vt[:, c * KCOLS: c * KCOLS + 256], in_=vt[:, c * KCOLS + 1024: c * KCOLS + 1280]),
                           waits=G("vt_rw"), inc=vE)

            kwaits = [wk_done[dd] for dd in range(DK)] + [(vE, xnf_all)]
            kv_ready = 0
            for c in range(NKV):
                for st2 in range(2):
                    bi = (c * 2 + st2) % 2
                    bank = pbank[bi]
                    mm = None
                    for dd in range(DK):
                        mm = p.emit("tensor", lambda h, dd=dd, c=c, st2=st2, bank=bank:
                                    h.matmul(out=bank, lhsT=wk_tile(dd)[:, c * 128:(c + 1) * 128],
                                             rhs=xn_f(dd)[:, st2 * 512:(st2 + 1) * 512],
                                             start=(dd == 0), stop=(dd == DK - 1)),
                                    waits=(kwaits + G(BK[bi])) if dd == 0 else [],
                                    inc=tE if dd == DK - 1 else None)
                    ev = p.emit("vector", lambda h, c=c, st2=st2, bank=bank: h.tensor_copy(
                        out=kf[:, c * KCOLS + 256 + st2 * 512: c * KCOLS + 256 + (st2 + 1) * 512], in_=bank),
                                waits=[(tE, mm), (vE, tail_done)] + G("kf_rw"), inc=vE)
                    setg(BK[bi], vE, ev)
                    kv_ready = ev
                if seq_start:
                    p.emit("vector", lambda h: h.tensor_copy(out=jnk, in_=epsc))
                    kv_ready = p.emit("vector", lambda h, c=c: h.tensor_copy(
                        out=ksink[:, c * 4:(c + 1) * 4], in_=kf[:, c * KCOLS + 256: c * KCOLS + 260]),
                        waits=G("kf_rw"), inc=vE)

            vt3 = vt.rearrange("p (c n) -> p c n", n=KCOLS)
            vwaits = [wv_done[dd] for dd in range(DK)] + [(vE, xnf_all)]
            for qt in range(NQT):
                bi = 2 + qt % 2
                bank = pbank[bi]
                mm = None
                for dd in range(DK):
                    mm = p.emit("tensor", lambda h, dd=dd, qt=qt, bank=bank:
                                h.matmul(out=bank, lhsT=xn_f(dd)[:, qt * 128:(qt + 1) * 128],
                                         rhs=wv_tile(dd),
                                         start=(dd == 0), stop=(dd == DK - 1)),
                                waits=(vwaits + G(BK[bi])) if dd == 0 else [],
                                inc=tE if dd == DK - 1 else None)
                ev = p.emit("vector", lambda h, qt=qt, bank=bank: h.tensor_copy(
                    out=vt3[:, :, (2 + qt) * 128:(3 + qt) * 128], in_=bank),
                            waits=[(tE, mm), (vE, tail_done)] + G("vt_rw"), inc=vE)
                setg(BK[bi], vE, ev)
                if seq_start and qt == 0:
                    p.emit("vector", lambda h: h.tensor_copy(out=jnk, in_=epsc))
                    p.emit("vector", lambda h: h.tensor_copy(
                        out=vsink.rearrange("p (c n) -> p c n", n=128)[0:4, :, :],
                        in_=vt3[0:4, :, 256:384]),
                           waits=[(vE, ev)] + G("vt_rw"), inc=vE)
            v_all = vE.n
            kv_ready = max(kv_ready, v_all)

            if maxphase < 2:
                continue
            qwaits = [wq_done[dd] for dd in range(DK)] + [(vE, xnf_all)]
            kv_mm_end = tE.n
            qf_ready = 0
            for hh in range(NH):
                for st2 in range(2):
                    bi = (hh * 2 + st2) % 2
                    bank = pbank[bi]
                    mm = None
                    for dd in range(DK):
                        mm = p.emit("tensor", lambda h, dd=dd, hh=hh, st2=st2, bank=bank:
                                    h.matmul(out=bank, lhsT=w_tile(dd)[:, hh * 128:(hh + 1) * 128],
                                             rhs=xn_f(dd)[:, st2 * 512:(st2 + 1) * 512],
                                             start=(dd == 0), stop=(dd == DK - 1)),
                                    waits=(qwaits + G(BK[bi])) if dd == 0 else [],
                                    inc=tE if dd == DK - 1 else None)
                    qf_ready = p.emit("vector", lambda h, hh=hh, st2=st2, bank=bank: h.tensor_scalar_mul(
                        out=arena_q[:, hh * 1024 + st2 * 512: hh * 1024 + (st2 + 1) * 512],
                        in0=bank, scalar1=SCALE),
                                waits=[(tE, mm), (tE, kv_mm_end),
                                       (dO, spill_record.get("tk", 0)), (dO, spill_record.get("tv", 0))]
                                      + G(BK[bi] + "x"), inc=vE)
                    setg(BK[bi], vE, qf_ready)
            q_mm_end = tE.n
            if first:
                setg("xnt0w", vE, xnf_all)
                setg("prepm", tE, q_mm_end)
                emit_ffn_prep()
                setg("vt_rw", tE, tE.n)

            if maxphase < 3:
                continue
            for t in range(16):
                setg(f"tq{t}", tE, q_mm_end)
            wo_done = load_weight(w_tile, 16, 2048, woT, wobf, "tq",
                                  extra=[(dO, spill_record.get("tq", 0))])

            xrel = xload
            emit_xload(0, x)
            emit_xload(1, x)

            misc_attn = (tE, q_mm_end)
            if first:
                misc_attn = (dO, prepv['wscl_spill'])
            pend_o = None
            o_state = {}

            def emit_o_slice(oqt, sl, ring_ready):
                bi4 = 6 + sl % 2
                bank5 = pbank[bi4]
                mm = None
                for hh in range(NH):
                    mm = p.emit("tensor", lambda h, hh=hh, sl=sl, bank5=bank5, oqt=oqt:
                                h.matmul(out=bank5, lhsT=ring[oqt % 2][:, hh * 128:(hh + 1) * 128],
                                         rhs=w_tile(hh)[:, sl * 512:(sl + 1) * 512],
                                         start=(hh == 0), stop=(hh == NH - 1)),
                                waits=([(vE, ring_ready)] + [wo_done[t] for t in range(16)]
                                       + G(BK[bi4])) if hh == 0 else [],
                                inc=tE if hh == NH - 1 else None)
                xv = p.emit("vector", lambda h, sl=sl, oqt=oqt, bank5=bank5: h.tensor_tensor(
                    out=xps[oqt % 2][:, sl * 512:(sl + 1) * 512], in0=bank5,
                    in1=xstage[oqt % 2][:, sl * 512:(sl + 1) * 512], op=ALU.add),
                            waits=[(tE, mm), (dG, xrel[oqt])]
                                  + (G(f"xps{oqt % 2}") if sl == 0 else [])
                                  + ([misc_attn] if oqt <= 1 else []),
                            inc=vE)
                setg(BK[bi4], vE, xv)
                o_state[oqt] = xv

            def emit_o_finish(oqt, tok0=tok0):
                xv_last = o_state.pop(oqt)
                setg(f"ring{oqt % 2}", tE, tE.n)
                setg(f"xst{oqt % 2}", vE, xv_last)
                if oqt + 2 < NQT:
                    emit_xload(oqt + 2, x)
                sp = p.emit("sync", lambda h, oqt=oqt, tok0=tok0: h.dma_start(
                    out=xp[tok0 + oqt * 128: tok0 + (oqt + 1) * 128, :], in_=xps[oqt % 2]),
                            waits=[(vE, xv_last)], inc=dO, inck=16)
                setg(f"xps{oqt % 2}", dO, sp)

            for qt in range(NQT):
                g = (s % 2) * NQT + qt
                if g == 0:
                    W, nch, has_sink, moff = 128, 1, False, 0
                elif g == 1:
                    W, nch, has_sink, moff = 256, 2, False, 128
                else:
                    W, nch, has_sink, moff = 384, 3, True, (384 if g == 2 else 772)
                kstart = 256 if g <= 1 else qt * 128
                wtot = W + (4 if has_sink else 0)
                nall = nch + (1 if has_sink else 0)

                st = {}
                pend_pts = []
                group_ev = {}

                def emit_pt(u, qt=qt, nch=nch, has_sink=has_sink, nall=nall):
                    kv, j = u // 4, u % 4
                    dg = st[u]
                    bi2 = 2 + u % 2
                    bank2 = pbank[bi2]
                    for cidx in range(nch):
                        p.emit("tensor", lambda h, cidx=cidx, u=u, bank2=bank2:
                               h.matmul(out=bank2[:, cidx * 128:(cidx + 1) * 128],
                                        lhsT=exb[u % 4][:, cidx * 128:(cidx + 1) * 128],
                                        rhs=diag[u % 4], start=True, stop=True),
                               waits=([(gE, dg)] + G(BK[bi2])) if cidx == 0 else [],
                               inc=tE if (cidx == nch - 1 and not has_sink) else None)
                    if has_sink:
                        p.emit("tensor", lambda h, u=u, bank2=bank2:
                               h.matmul(out=bank2[0:4, 384:512],
                                        lhsT=exb[u % 4][:, 384:388], rhs=diag[u % 4],
                                        start=True, stop=True),
                               inc=tE)
                    ptv = tE.n
                    setg(f"exb{u % 4}", tE, ptv)
                    setg(f"diag{u % 4}", tE, ptv)
                    pt3 = ptbuf[kv % 2].rearrange("p (c n) -> p c n", n=512)
                    bank3 = bank2.rearrange("p (c n) -> p c n", n=128)
                    pe = p.emit("scalar", lambda h, j=j, pt3=pt3, bank3=bank3, nall=nall:
                                h.activation(out=pt3[:, 0:nall, j * 128:(j + 1) * 128],
                                             in_=bank3[:, 0:nall, :], func=AF.Identity, scale=1.0),
                                waits=[(tE, ptv)]
                                      + (G(f"ptbuf{kv % 2}") if j == 0 else [])
                                      + ([misc_attn] if (qt == 0 and kv < 2) else []),
                                inc=sE)
                    setg(BK[bi2], sE, pe)
                    st[u] = pe

                def emit_av(kv, qt=qt, g=g, nch=nch, has_sink=has_sink):
                    bi3 = 4
                    bank4 = pbank[bi3]
                    evs = max(st[kv * 4 + j] for j in range(4))
                    for cidx in range(nch):
                        blk = (2 + cidx) if g <= 1 else (qt + cidx)
                        p.emit("tensor", lambda h, cidx=cidx, kv=kv, blk=blk, bank4=bank4:
                               h.matmul(out=bank4,
                                        lhsT=vt[:, kv * KCOLS + blk * 128: kv * KCOLS + (blk + 1) * 128],
                                        rhs=ptbuf[kv % 2][:, cidx * 512:(cidx + 1) * 512],
                                        start=(cidx == 0), stop=(cidx == nch - 1 and not has_sink)),
                               waits=([(sE, evs), (vE, v_all)] + G(BK[bi3])) if cidx == 0 else [],
                               inc=tE if (cidx == nch - 1 and not has_sink) else None)
                    if has_sink:
                        p.emit("tensor", lambda h, kv=kv, bank4=bank4:
                               h.matmul(out=bank4, lhsT=vsink[0:4, kv * 128:(kv + 1) * 128],
                                        rhs=ptbuf[kv % 2][0:4, 1536:2048],
                                        start=False, stop=True),
                               inc=tE)
                    av_t = tE.n
                    setg(f"ptbuf{kv % 2}", tE, av_t)
                    rv = p.emit("vector", lambda h, kv=kv, qt=qt, bank4=bank4: h.tensor_copy(
                        out=ring[qt % 2][:, kv * 512:(kv + 1) * 512], in_=bank4),
                                waits=[(tE, av_t)] + (G(f"ring{qt % 2}") if kv == 0 else [])
                                      + ([misc_attn] if qt == 0 else []),
                                inc=vE)
                    setg(BK[bi3], vE, rv)
                    group_ev[kv] = rv

                for kv in range(NKV):
                    for j in range(4):
                        u = kv * 4 + j
                        hh = kv * 4 + j
                        bi = [0, 1, 5][u % 3]
                        bank = pbank[bi]
                        mm = p.emit("tensor", lambda h, hh=hh, kv=kv, bank=bank, W=W, kstart=kstart, qt=qt:
                                    h.matmul(out=bank[:, 0:W],
                                             lhsT=arena_q[:, hh * 1024 + qt * 128: hh * 1024 + qt * 128 + 128],
                                             rhs=kf[:, kv * KCOLS + kstart: kv * KCOLS + kstart + W],
                                             start=True, stop=False),
                                    waits=[(vE, qf_ready), (vE, kv_ready)] + G(BK[bi]))
                        if has_sink:
                            mm = p.emit("tensor", lambda h, hh=hh, kv=kv, bank=bank, qt=qt:
                                        h.matmul(out=bank[:, 384:388],
                                                 lhsT=arena_q[:, hh * 1024 + qt * 128: hh * 1024 + qt * 128 + 128],
                                                 rhs=ksink[:, kv * 4:(kv + 1) * 4],
                                                 start=False, stop=False))
                        mm = p.emit("tensor", lambda h, bank=bank, wtot=wtot, moff=moff:
                                    h.matmul(out=bank[:, 0:wtot], lhsT=idbuf,
                                             rhs=maskb[:, moff:moff + wtot],
                                             start=False, stop=True),
                                    waits=[(dL, c_consts)], inc=tE)
                        sc = tE.n
                        mx = p.emit("vector", lambda h, u=u, bank=bank, wtot=wtot: h.reduce_max(
                            out=negm[:, u % 4:u % 4 + 1], in_=bank[:, 0:wtot], axis=AX.X, negate=True),
                                    waits=[(tE, sc)] + G(f"negm{u % 4}"), inc=vE)
                        ex_ = p.emit("scalar", lambda h, u=u, bank=bank, wtot=wtot: h.activation(
                            out=exb[u % 4][:, 0:wtot], in_=bank[:, 0:wtot], func=AF.Exp,
                            bias=negm[:, u % 4:u % 4 + 1], scale=1.0,
                            accum_out=den[:, u % 4:u % 4 + 1]),
                                     waits=[(vE, mx)] + G(f"exb{u % 4}") + G(f"den{u % 4}"), inc=sE)
                        setg(BK[bi], sE, ex_)
                        setg(f"negm{u % 4}", sE, ex_)
                        def _recip(h, u=u):
                            with nc.allow_low_precision(reason="alpha in bf16 feeds bf16 matmul"):
                                return h.reciprocal(out=alphab[:, u % 4:u % 4 + 1],
                                                    in_=den[:, u % 4:u % 4 + 1])
                        rc = p.emit("vector", _recip, waits=[(sE, ex_)] + G(f"alf{u % 4}"), inc=vE)
                        setg(f"den{u % 4}", vE, rc)
                        dg = p.emit("gpsimd", lambda h, u=u: h.tensor_tensor(
                            out=diag[u % 4], in0=alphab[:, u % 4:u % 4 + 1].broadcast_to((128, 128)),
                            in1=idbuf, op=ALU.mult),
                                    waits=[(vE, rc), (dL, c_consts)] + G(f"diag{u % 4}"), inc=gE)
                        setg(f"alf{u % 4}", gE, dg)
                        st[u] = dg
                        pend_pts.append(u)
                        if len(pend_pts) > 3:
                            pu = pend_pts.pop(0)
                            emit_pt(pu)
                            if pu % 4 == 3:
                                emit_av(pu // 4)
                                if pend_o is not None:
                                    oqt, orr = pend_o
                                    emit_o_slice(oqt, pu // 4, orr)
                                    if pu // 4 == 3:
                                        emit_o_finish(oqt)
                                        pend_o = None
                for pu in pend_pts:
                    emit_pt(pu)
                    if pu % 4 == 3:
                        emit_av(pu // 4)
                        if pend_o is not None:
                            oqt, orr = pend_o
                            emit_o_slice(oqt, pu // 4, orr)
                            if pu // 4 == 3:
                                emit_o_finish(oqt)
                                pend_o = None
                pend_pts = []
                pend_o = (qt, max(group_ev.values()))
            oqt, orr = pend_o
            for sl in range(4):
                emit_o_slice(oqt, sl, orr)
            emit_o_finish(oqt)
            pend_o = None
            setg(f"xpsl{0}", dO, guard[f"xps{0}"][1])
            setg(f"xpsl{1}", dO, guard[f"xps{1}"][1])
            attn_te_end = tE.n

            if maxphase < 4:
                continue
            for t in range(16):
                setg(f"tq{t}", tE, attn_te_end)
            wf_done = []
            for t in range(16):
                w = p.emit("sync", lambda h, t=t: h.dma_start(out=w_tile(t), in_=wfbf[t * 128:(t + 1) * 128, :]),
                           waits=G(f"tq{t}") + [(dO, prepv['wf_spill_done']), (dO, spill_record.get("tq", 0))],
                           inc=dL, inck=16)
                wf_done.append((dL, w))
            setg("outst", dO, max(guard.get("xps0", (None, 0))[1], guard.get("xps1", (None, 0))[1]))
            wslw = p.emit("sync", lambda h: h.dma_start(out=wsb, in_=wscl),
                          waits=[(tE, attn_te_end), (dO, prepv['wscl_spill'])], inc=dL, inck=16)
            bblw = p.emit("sync", lambda h: h.dma_start(out=bib, in_=biasb),
                          waits=[(tE, attn_te_end)], inc=dL, inck=16)

            out_dma_last = 0
            emit_xload(0, xp, extra=G(f"xpsl0"), nb=3)
            emit_xload(1, xp, extra=G(f"xpsl1"), nb=3)
            emit_xload(2, xp, extra=G(f"xpsl0") + list(wo_done[-2:]), nb=3)
            ffn_lastev = {}
            ffn_sq2 = {}

            def ffn_stats(qt):
                pb = (qt % 3) * 4
                xw = xload[qt]
                v1 = p.emit("vector", lambda h, qt=qt, pb=pb: h.reduce_sum(
                    out=statc[:, pb + 3:pb + 4], in_=xstage[qt % 3], axis=AX.X),
                            waits=[(dG, xw)] + G(f"mf{qt % 3}"), inc=vE)
                sq = None
                for j in range(4):
                    sq = p.emit("scalar", lambda h, j=j, qt=qt: h.activation(
                        out=sqs, in_=xstage[qt % 3][:, j * 512:(j + 1) * 512], func=AF.Square, scale=1.0,
                        accum_out=statc[:, 16 + j:17 + j]),
                        waits=[(dG, xw)] + (G("sqaccF") if j == 0 else []), inc=sE)
                vmx = p.emit("vector", lambda h, qt=qt: h.reduce_max(
                    out=statc[:, 13:14], in_=xstage[qt % 3], axis=AX.X),
                             waits=[(dG, xw)] + G("col13"), inc=vE)
                vmn = p.emit("vector", lambda h, qt=qt: h.tensor_reduce(
                    out=statc[:, 14:15], in_=xstage[qt % 3], axis=AX.X, op=ALU.min),
                             waits=[(dG, xw)], inc=vE)
                v2 = p.emit("vector", lambda h: h.reduce_sum(out=statc[:, 15:16], in_=statc[:, 16:20], axis=AX.X),
                            waits=[(sE, sq)], inc=vE)
                setg("sqaccF", vE, v2)
                v3 = p.emit("vector", lambda h, pb=pb: h.tensor_scalar_mul(
                    out=statc[:, pb + 3:pb + 4], in0=statc[:, pb + 3:pb + 4], scalar1=INVD), inc=vE)
                v4 = p.emit("vector", lambda h: h.tensor_scalar_mul(
                    out=statc[:, 15:16], in0=statc[:, 15:16], scalar1=INVD), inc=vE)
                v5 = p.emit("vector", lambda h, pb=pb: h.tensor_tensor(
                    out=statc[:, 12:13], in0=statc[:, pb + 3:pb + 4], in1=statc[:, pb + 3:pb + 4],
                    op=ALU.mult),
                            waits=G("col12"), inc=vE)
                v7 = p.emit("vector", lambda h, pb=pb: h.tensor_tensor(
                    out=statc[:, 13:14], in0=statc[:, 13:14], in1=statc[:, pb + 3:pb + 4],
                    op=ALU.subtract),
                            waits=[(vE, vmx)], inc=vE)
                v6 = p.emit("vector", lambda h: h.scalar_tensor_tensor(
                    out=statc[:, 12:13], in0=statc[:, 12:13], scalar=-1.0, in1=statc[:, 15:16],
                    op0=ALU.mult, op1=ALU.add), inc=vE)
                v8 = p.emit("vector", lambda h, pb=pb: h.tensor_tensor(
                    out=statc[:, 14:15], in0=statc[:, pb + 3:pb + 4], in1=statc[:, 14:15],
                    op=ALU.subtract), inc=vE)
                s1 = p.emit("scalar", lambda h: h.activation(
                    out=statc[:, 12:13], in_=statc[:, 12:13], func=AF.Ln, bias=epsc, scale=1.0),
                            waits=[(vE, v6)], inc=sE)
                setg("col12", sE, s1)
                s2 = p.emit("scalar", lambda h: h.activation(
                    out=statc[:, 20:21], in_=statc[:, 12:13], func=AF.Exp, scale=-0.5),
                            waits=G("rs20"), inc=sE)
                p.emit("vector", lambda h: h.tensor_copy(out=jnk, in_=epsc))
                v9 = p.emit("vector", lambda h: h.tensor_tensor(
                    out=statc[:, 13:14], in0=statc[:, 13:14], in1=statc[:, 14:15], op=ALU.max), inc=vE)
                p.emit("vector", lambda h: h.tensor_copy(out=jnk, in_=epsc))
                v10 = p.emit("vector", lambda h: h.tensor_tensor(
                    out=statc[:, 13:14], in0=statc[:, 13:14], in1=statc[:, 20:21], op=ALU.mult),
                             waits=[(sE, s2)], inc=vE)
                p.emit("vector", lambda h: h.tensor_copy(out=jnk, in_=epsc))
                v11 = p.emit("vector", lambda h, pb=pb: h.tensor_scalar(
                    out=statc[:, pb:pb + 1], in0=statc[:, 13:14], scalar1=INV127, scalar2=1e-12,
                    op0=ALU.mult, op1=ALU.max), inc=vE)
                v12 = p.emit("vector", lambda h: h.tensor_copy(out=jnk, in_=epsc), inc=vE)
                v13 = p.emit("vector", lambda h, pb=pb: h.reciprocal(
                    out=statc[:, pb + 1:pb + 2], in_=statc[:, pb:pb + 1]),
                             waits=G(f"sbq{qt % 3}"), inc=vE)
                p.emit("vector", lambda h: h.tensor_copy(out=jnk, in_=epsc))
                v14 = p.emit("vector", lambda h, pb=pb: h.tensor_tensor(
                    out=statc[:, pb + 1:pb + 2], in0=statc[:, pb + 1:pb + 2],
                    in1=statc[:, 20:21], op=ALU.mult), inc=vE)
                setg("col13", vE, v14)
                setg("rs20", vE, v14)
                p.emit("vector", lambda h: h.tensor_copy(out=jnk, in_=epsc))
                v15 = p.emit("vector", lambda h, pb=pb: h.scalar_tensor_tensor(
                    out=statc[:, pb + 2:pb + 3], in0=statc[:, pb + 3:pb + 4], scalar=-1.0,
                    in1=statc[:, pb + 1:pb + 2], op0=ALU.mult, op1=ALU.mult), inc=vE)
                sq1 = p.emit("scalar", lambda h, qt=qt, pb=pb: h.activation(
                    out=xq8, in_=xstage[qt % 3], func=AF.Identity,
                    scale=statc[:, pb + 1:pb + 2], bias=statc[:, pb + 2:pb + 3]),
                             waits=[(vE, v15)] + G("xq8"), inc=sE)
                setg(f"sbq{qt % 3}", sE, sq1)
                setg(f"mf{qt % 3}", sE, sq1)
                sq2 = p.emit("scalar", lambda h, qt=qt: h.activation(out=xqtb[qt % 2], in_=xq8,
                                                                     func=AF.Identity, scale=1.0),
                             waits=G(f"xqt{qt % 2}") + G(f"xpsl{1}")
                                   + ([] if qt else (G("misc_ffn") + G("wsrow_rd"))), inc=sE)
                setg("xq8", sE, sq2)
                ffn_sq2[qt] = sq2

            def ffn_tr(qt):
                sq2 = ffn_sq2.pop(qt)
                lastev = None
                for half in range(4):
                    bi = 2 + half % 2
                    bank = pbank[bi]
                    tw = None
                    for j in range(4):
                        dd = half * 4 + j
                        tw = p.emit("tensor", lambda h, dd=dd, j=j, bank=bank, qt=qt:
                                    h.matmul(out=bank[:, j * 128:(j + 1) * 128],
                                             lhsT=xqtb[qt % 2][:, dd * 128:(dd + 1) * 128], rhs=idbuf,
                                             start=True, stop=True),
                                    waits=([(sE, sq2)] + G(BK[bi])) if j == 0 else [],
                                    inc=tE if j == 3 else None)
                    aq3 = arena_q.rearrange("p (c n) -> p c n", n=1024)
                    ev = p.emit("vector", lambda h, half=half, bank=bank, qt=qt, aq3=aq3:
                                h.tensor_copy(out=aq3[:, half * 4:(half + 1) * 4, qt * 128:(qt + 1) * 128],
                                              in_=bank),
                                waits=[(tE, tw), (tE, attn_te_end),
                                       (dO, spill_record.get("tv", 0))], inc=vE)
                    setg(BK[bi], vE, ev)
                    lastev = ev
                setg(f"xqt{qt % 2}", vE, lastev)
                ffn_lastev[qt] = lastev

            def ffn_back(qt, mid=None):
                base = (qt % 3) * 4
                lastev = ffn_lastev.pop(qt)
                mmv = [None] * 4
                e1v = [None] * 4

                def ffn_mm(sl, qt=qt, lastev=lastev):
                    bi4 = 6 + sl % 2
                    bank5 = pbank[bi4]
                    mm = None
                    for dd in range(DK):
                        mm = p.emit("tensor", lambda h, dd=dd, sl=sl, bank5=bank5, qt=qt:
                                    h.matmul(out=bank5,
                                             lhsT=arena_q[:, dd * 1024 + qt * 128: dd * 1024 + qt * 128 + 128],
                                             rhs=w_tile(dd)[:, sl * 512:(sl + 1) * 512],
                                             start=(dd == 0), stop=(dd == DK - 1)),
                                    waits=([(vE, lastev), (dL, wslw), (dL, bblw)]
                                           + [wf_done[t] for t in range(16)]
                                           + G(BK[bi4])) if dd == 0 else [],
                                    inc=tE if dd == DK - 1 else None)
                    mmv[sl] = mm

                def ffn_e1(sl, base=base):
                    bi4 = 6 + sl % 2
                    e1v[sl] = p.emit("vector", lambda h, sl=sl, base=base: h.scalar_tensor_tensor(
                        out=outst[:, sl * 512:(sl + 1) * 512], in0=pbank[6 + sl % 2],
                        scalar=statc[:, base:base + 1], in1=wsb[:, sl * 512:(sl + 1) * 512],
                        op0=ALU.mult, op1=ALU.mult),
                                waits=[(tE, mmv[sl]), (dL, wslw)]
                                      + (G("outst") if sl == 0 else []), inc=vE)
                    setg(BK[bi4], vE, e1v[sl])

                ffn_mm(0); ffn_mm(1)
                ffn_e1(0)
                if mid is not None:
                    mid()
                ffn_mm(2)
                ffn_e1(1)
                ffn_mm(3)
                ffn_e1(2); ffn_e1(3)
                for sl in range(4):
                    p.emit("vector", lambda h, sl=sl: h.tensor_tensor(
                        out=tmpv[sl], in0=outst[:, sl * 512:(sl + 1) * 512],
                        in1=bib[:, sl * 512:(sl + 1) * 512], op=ALU.add),
                           waits=[(dL, bblw)] + (G("tmpv") if sl == 0 else []), inc=vE)
                out_writes = 0
                for sl in range(4):
                    out_writes = p.emit("vector", lambda h, sl=sl, qt=qt: h.tensor_tensor(
                        out=outst[:, sl * 512:(sl + 1) * 512], in0=tmpv[sl],
                        in1=xstage[qt % 3][:, sl * 512:(sl + 1) * 512], op=ALU.add), inc=vE)
                setg("tmpv", vE, out_writes)
                setg(f"xst{qt % 3}", vE, out_writes)
                if qt + 3 < NQT:
                    emit_xload(qt + 3, xp, extra=G(f"xpsl{(qt + 3) % 2}"), nb=3)
                ow = p.emit("sync", lambda h, qt=qt, tok0=tok0: h.dma_start(
                    out=y[tok0 + qt * 128: tok0 + (qt + 1) * 128, :], in_=outst),
                            waits=[(vE, out_writes)], inc=dO, inck=16)
                setg("outst", dO, ow)
                return ow

            ffn_stats(0)
            ffn_stats(1)
            ffn_tr(0)
            for qt in range(NQT):
                if qt + 2 < NQT:
                    ffn_stats(qt + 2)
                if qt + 1 < NQT:
                    out_dma_last = ffn_back(qt, mid=lambda qt=qt: ffn_tr(qt + 1))
                else:
                    out_dma_last = ffn_back(qt)
            ffn_te_end = tE.n
            setg("kf_rw", tE, attn_te_end)
            setg("vt_rw", tE, attn_te_end)
            setg("misc_ln", dO, out_dma_last)
            setg("xps0", dO, out_dma_last)
            setg("xps1", dO, out_dma_last)
            setg("outst", dO, out_dma_last)
            setg("misc_ffn", dO, guard.get("xps0", (dO, 0))[1])
            for t in range(16):
                setg(f"tq{t}", tE, ffn_te_end)
                setg(f"tk{t}", tE, ffn_te_end)
                setg(f"tv{t}", tE, ffn_te_end)
            setg("awq", tE, ffn_te_end)
            setg(BK[0] + "x", tE, ffn_te_end)
            setg(BK[1] + "x", tE, ffn_te_end)

        if debug:
            for dst, src_ in [(dkf, kf), (dvt, vt), (dq, arena_q), (dm, arena_m)]:
                p.emit("sync", lambda h, d=dst, s_=src_: h.dma_start(out=d, in_=s_),
                       waits=[(dO, dO.n), (tE, tE.n), (vE, vE.n), (sE, sE.n)], inc=dO, inck=16)
            p.emit("sync", lambda h: h.dma_start(out=dxp, in_=xp), inc=dO, inck=16)
            p.emit("sync", lambda h: h.dma_start(out=dstat, in_=statc[:, 0:16]), inc=dO, inck=16)
            p.emit("sync", lambda h: h.dma_start(out=dxs0, in_=xstage[0]), inc=dO, inck=16)
            p.emit("sync", lambda h: h.dma_start(out=dxs1, in_=xstage[1]), inc=dO, inck=16)
            p.emit("sync", lambda h: h.dma_start(out=dxnt1, in_=xnt[1]), inc=dO, inck=16)
            p.emit("sync", lambda h: h.dma_start(out=dwk, in_=wkbf), inc=dO, inck=16)
            p.emit("sync", lambda h: h.dma_start(out=dwo, in_=wobf), inc=dO, inck=16)
            p.emit("sync", lambda h: h.dma_start(out=dwf, in_=wfbf), inc=dO, inck=16)
            p.emit("sync", lambda h: h.dma_start(out=dws, in_=wscl), inc=dO, inck=16)
        final_dO = dO.n

        with nc.Block() as block:
            @block.sync
            def _(sync):
                for f in p.q["sync"]:
                    f(sync)
                sync.wait_ge(dO.h, final_dO)

            @block.tensor
            def _(tensor):
                for f in p.q["tensor"]:
                    f(tensor)

            @block.vector
            def _(vector):
                for f in p.q["vector"]:
                    f(vector)

            @block.scalar
            def _(scalar):
                for f in p.q["scalar"]:
                    f(scalar)

            @block.gpsimd
            def _(gpsimd):
                for f in p.q["gpsimd"]:
                    f(gpsimd)

    return nc


def _build_consts():
    r = np.arange(128)[:, None]
    j1 = np.arange(128)[None, :]
    j2 = np.arange(256)[None, :]
    j3 = np.arange(384)[None, :]
    m0 = (j1 <= r)
    m1 = (j2 <= 128 + r)
    band = (j3 >= r + 1) & (j3 < r + 257)
    s2 = (np.arange(4)[None, :] < np.minimum(4, r + 1))
    sS = np.ones((128, 4), bool)
    m2 = np.concatenate([band, s2], axis=1)
    mS = np.concatenate([band, sS], axis=1)
    full = np.concatenate([m0, m1, m2, mS], axis=1)
    masks = np.where(full, np.float32(0.0), np.float32(-1e30)).astype(ml_dtypes.bfloat16)
    idf = np.eye(128, dtype=np.float32)
    idb = np.eye(128).astype(ml_dtypes.bfloat16)
    return masks, idf, idb


def kernel(L0_attn_x, L0_attn_wq, L0_attn_wk, L0_attn_wv, L0_attn_wo, L0_ffn_weight, L0_ffn_bias):
    x = np.ascontiguousarray(np.asarray(L0_attn_x, dtype=np.float32))
    wq = np.asarray(L0_attn_wq, np.float32)
    wk = np.asarray(L0_attn_wk, np.float32)
    wv = np.asarray(L0_attn_wv, np.float32)
    wo = np.asarray(L0_attn_wo, np.float32)
    wf = np.ascontiguousarray(np.asarray(L0_ffn_weight, np.float32))
    bias = np.asarray(L0_ffn_bias, np.float32)

    wqT = np.ascontiguousarray(wq.T)
    wkT = np.ascontiguousarray(wk.T)
    wvT = np.ascontiguousarray(wv.T)
    woT = np.ascontiguousarray(wo.T)
    biasb = np.ascontiguousarray(np.broadcast_to(bias[None, :], (128, D)))
    masks, idf, idb = _build_consts()

    nc = build_nc()
    in_maps = []
    for i in range(NCORES):
        xi = np.ascontiguousarray(x[2 * i: 2 * i + 2].reshape(TOK, D))
        in_maps.append(dict(x=xi, wqT=wqT, wkT=wkT, wvT=wvT, woT=woT, wfO=wf,
                            biasb=biasb, idf=idf, idb=idb, masks=masks))
    res = run_bass_kernel_spmd(nc, in_maps, core_ids=list(range(NCORES)))
    out = np.empty((B, T, D), np.float32)
    for i in range(NCORES):
        out[2 * i: 2 * i + 2] = res.results[i]["y"].reshape(2, T, D)
    return out


# version 17 (final)
# speedup vs baseline: 1.0198x; 1.0198x over previous
import math
import numpy as np
import ml_dtypes
import concourse.bass as bass
import concourse.mybir as mybir
from concourse.bass_utils import run_bass_kernel_spmd
from contextlib import ExitStack

F32 = mybir.dt.float32
BF16 = mybir.dt.bfloat16
I8 = mybir.dt.int8
AF = mybir.ActivationFunctionType
ALU = mybir.AluOpType
AX = mybir.AxisListType

NCORES = 8
B, T, D = 16, 2048, 2048
TOK = (B // NCORES) * T
STRIPE = 1024
NSTRIPE = TOK // STRIPE
NQT = STRIPE // 128
NH, NKV, HD = 16, 4, 128
DK = D // 128
SCALE = 1.0 / math.sqrt(HD)
EPS = 1e-5
INV127 = float(np.float32(1.0 / 127.0))
INVD = 1.0 / 2048.0
KCOLS = 1280


class Sem:
    def __init__(self, stk, nc, name):
        self.h = stk.enter_context(nc.semaphore(name))
        self.n = 0


class Prog:
    ENGS = ("sync", "tensor", "vector", "scalar", "gpsimd")

    def __init__(self):
        self.q = {e: [] for e in self.ENGS}
        self.last_wait = {e: {} for e in self.ENGS}

    def emit(self, eng, fn, waits=(), inc=None, inck=1):
        lw = self.last_wait[eng]
        rw = []
        for sem, val in waits:
            if sem is None or val is None or val <= 0:
                continue
            if lw.get(id(sem), -1) >= val:
                continue
            lw[id(sem)] = val
            rw.append((sem, val))

        def do(h, fn=fn, rw=tuple(rw), inc=inc, inck=inck):
            for s, v in rw:
                h.wait_ge(s.h, v)
            r = fn(h)
            if inc is not None:
                r.then_inc(inc.h, inck)
        self.q[eng].append(do)
        if inc is not None:
            inc.n += inck
            return inc.n
        return None


def build_nc(nstripe=NSTRIPE, debug=False, maxphase=9):
    nc = bass.Bass("TRN2", target_bir_lowering=False)

    x = nc.dram_tensor("x", [TOK, D], F32, kind="ExternalInput").ap()
    wqT = nc.dram_tensor("wqT", [D, D], F32, kind="ExternalInput").ap()
    wkT = nc.dram_tensor("wkT", [D, 512], F32, kind="ExternalInput").ap()
    wvT = nc.dram_tensor("wvT", [D, 512], F32, kind="ExternalInput").ap()
    woT = nc.dram_tensor("woT", [D, D], F32, kind="ExternalInput").ap()
    wfO = nc.dram_tensor("wfO", [D, D], F32, kind="ExternalInput").ap()
    biasb = nc.dram_tensor("biasb", [128, D], F32, kind="ExternalInput").ap()
    idf = nc.dram_tensor("idf", [128, 128], F32, kind="ExternalInput").ap()
    idb = nc.dram_tensor("idb", [128, 128], BF16, kind="ExternalInput").ap()
    masksrc = nc.dram_tensor("masks", [128, 1160], BF16, kind="ExternalInput").ap()
    y = nc.dram_tensor("y", [TOK, D], F32, kind="ExternalOutput").ap()
    xp = nc.dram_tensor("xp", [TOK, D], F32, kind="Internal").ap()
    wqbf = nc.dram_tensor("wqbf", [D, D], BF16, kind="Internal").ap()
    wkbf = nc.dram_tensor("wkbf", [D, 512], BF16, kind="Internal").ap()
    wvbf = nc.dram_tensor("wvbf", [D, 512], BF16, kind="Internal").ap()
    wobf = nc.dram_tensor("wobf", [D, D], BF16, kind="Internal").ap()
    wfbf = nc.dram_tensor("wfbf", [D, D], BF16, kind="Internal").ap()
    wscl = nc.dram_tensor("wscl", [128, D], F32, kind="Internal").ap()
    if debug:
        dxp = nc.dram_tensor("dxp", [TOK, D], F32, kind="ExternalOutput").ap()
        dkf = nc.dram_tensor("dkf", [128, NKV * KCOLS], BF16, kind="ExternalOutput").ap()
        dvt = nc.dram_tensor("dvt", [128, NKV * KCOLS], BF16, kind="ExternalOutput").ap()
        dq = nc.dram_tensor("dq", [128, 16384], BF16, kind="ExternalOutput").ap()
        dm = nc.dram_tensor("dm", [128, 16384], BF16, kind="ExternalOutput").ap()
        dwf = nc.dram_tensor("dwf", [D, D], BF16, kind="ExternalOutput").ap()
        dws = nc.dram_tensor("dws", [128, D], F32, kind="ExternalOutput").ap()
        dstat = nc.dram_tensor("dstat", [128, 16], F32, kind="ExternalOutput").ap()
        dxs0 = nc.dram_tensor("dxs0", [128, D], F32, kind="ExternalOutput").ap()
        dxs1 = nc.dram_tensor("dxs1", [128, D], F32, kind="ExternalOutput").ap()
        dxnt1 = nc.dram_tensor("dxnt1", [128, D], BF16, kind="ExternalOutput").ap()
        dwk = nc.dram_tensor("dwk", [D, 512], BF16, kind="ExternalOutput").ap()
        dwo = nc.dram_tensor("dwo", [D, D], BF16, kind="ExternalOutput").ap()

    with ExitStack() as stk:
        def S(*a, **k):
            return stk.enter_context(nc.sbuf_tensor(*a, **k)).ap()

        arena_w = S("arena_w", [128, 16 * 2048], BF16)
        arena_q = S("arena_q", [128, 16 * 1024], BF16)
        arena_m = S("arena_m", [128, 16 * 1024], BF16)
        am32 = arena_m.bitcast(F32)
        kf = S("kf", [128, NKV * KCOLS], BF16)
        vt = S("vt", [128, NKV * KCOLS], BF16)
        ksink = S("ksink", [128, 16], BF16)
        vsink = S("vsink", [128, NKV * 128], BF16)
        xstage = [S(f"xstage{i}", [128, D], F32) for i in range(2)]
        xnt = [S(f"xnt{i}", [128, D], BF16) for i in range(2)]
        wstage = [S(f"wstage{i}", [128, D], F32) for i in range(2)]
        xstage.append(wstage[0])
        sqs = S("sqs", [128, 512], F32)
        exb = [S(f"exb{i}", [128, 388], BF16) for i in range(4)]
        pmb = [S(f"pmb{i}", [128, 388], BF16) for i in range(4)]
        diag = [S(f"diag{i}", [128, 128], BF16) for i in range(4)]
        negm = S("negm", [128, 4], F32)
        den = S("den", [128, 4], F32)
        alphab = S("alphab", [128, 4], BF16)
        statc = S("statc", [128, 24], F32)
        epsc = S("epsc", [128, 1], F32)
        jnk = S("jnk", [128, 1], F32)
        onesr = S("onesr", [1, 128], F32)
        maskb = S("maskb", [128, 1160], BF16)
        idbuf = S("idbuf", [128, 128], BF16)
        idfb = S("idfb", [128, 128], F32)
        wmaxt = S("wmaxt", [128, 16], F32)
        wrec = S("wrec", [128, 16], F32)
        wsclT = S("wsclT", [128, 16], F32)
        w8st = S("w8st", [128, D], I8)
        xq8 = w8st
        tmpb = None

        def xn_f(dd):
            return arena_m[:, dd * 1024:(dd + 1) * 1024]
        am3 = arena_m.rearrange("p (c n) -> p c n", n=1024)
        ptbuf = [arena_m[:, 0:2048], arena_m[:, 2048:4096]]
        ring = [arena_m[:, 4096:6144], arena_m[:, 6144:8192]]
        xps = [am32[:, 4096:6144], am32[:, 6144:8192]]
        wsb = am32[:, 0:2048]
        bib = am32[:, 2048:4096]
        outst = am32[:, 4096:6144]
        xqtb = [arena_m[:, 12288:14336], arena_m[:, 14336:16384]]

        wsrow = am32[0:1, 6144:8192]
        tmpv = [xnt[sl // 2].bitcast(F32)[:, (sl % 2) * 512:(sl % 2 + 1) * 512] for sl in range(4)]
        pbank = [stk.enter_context(nc.psum_tensor(f"ps{i}", [128, 512], F32)).ap() for i in range(8)]

        dL = Sem(stk, nc, "dL")
        dO = Sem(stk, nc, "dO")
        dG = Sem(stk, nc, "dG")
        tE = Sem(stk, nc, "tE")
        vE = Sem(stk, nc, "vE")
        sE = Sem(stk, nc, "sE")
        gE = Sem(stk, nc, "gE")
        p = Prog()

        guard = {}

        def G(key):
            g = guard.get(key)
            return [g] if g else []

        def setg(key, sem, val):
            guard[key] = (sem, val)

        BK = [f"bank{i}" for i in range(8)]

        for dst, src in [(idfb, idf), (idbuf, idb), (maskb, masksrc)]:
            p.emit("sync", lambda h, d=dst, s=src: h.dma_start(out=d, in_=s), inc=dL, inck=16)
        c_consts = dL.n
        p.emit("vector", lambda h: h.memset(epsc, EPS))
        p.emit("vector", lambda h: h.memset(onesr, 1.0))
        for i in range(4):
            p.emit("vector", lambda h, i=i: h.memset(diag[i], 0.0))

        prepv = {}

        def emit_ffn_prep():
            kst = [arena_m[:, 4096:4608], arena_m[:, 4608:5120]]
            wfb3 = wfbf.rearrange("(c p) n -> p c n", p=128)
            last_sp = 0
            for t in range(16):
                w = p.emit("sync", lambda h, t=t: h.dma_start(out=wstage[t % 2], in_=wfO[t * 128:(t + 1) * 128, :]),
                           waits=G(f"wst{t % 2}"), inc=dL, inck=16)
                vm = p.emit("vector", lambda h, t=t: h.reduce_max(out=wmaxt[:, t:t + 1], in_=wstage[t % 2],
                                                                  axis=AX.X, apply_absolute_value=True),
                           waits=[(dL, w)], inc=vE)
                p.emit("vector", lambda h: h.tensor_copy(out=jnk, in_=epsc))
                p.emit("vector", lambda h, t=t: h.tensor_scalar(
                    out=wsclT[:, t:t + 1], in0=wmaxt[:, t:t + 1], scalar1=1e-8, scalar2=INV127,
                    op0=ALU.max, op1=ALU.mult), inc=vE)
                p.emit("vector", lambda h: h.tensor_copy(out=jnk, in_=epsc))
                def _wrec(h, t=t):
                    return h.reciprocal(out=wrec[:, t:t + 1], in_=wsclT[:, t:t + 1])
                vrec = p.emit("vector", _wrec, inc=vE)
                s1 = p.emit("scalar", lambda h, t=t: h.activation(out=w8st, in_=wstage[t % 2], func=AF.Identity,
                                                                  scale=wrec[:, t:t + 1]),
                            waits=[(vE, vrec)] + G("w8st"), inc=sE)
                setg(f"wst{t % 2}", sE, s1)
                s2 = p.emit("scalar", lambda h: h.activation(out=xnt[0], in_=w8st, func=AF.Identity, scale=1.0),
                            waits=G("xnt0w"), inc=sE)
                setg("w8st", sE, s2)
                for half in range(4):
                    bi = 6 + half % 2
                    bank = pbank[bi]
                    tw = None
                    for j in range(4):
                        dd = half * 4 + j
                        tw = p.emit("tensor", lambda h, dd=dd, j=j, bank=bank:
                                    h.matmul(out=bank[:, j * 128:(j + 1) * 128],
                                             lhsT=xnt[0][:, dd * 128:(dd + 1) * 128], rhs=idbuf,
                                             start=True, stop=True),
                                    waits=([(sE, s2), (dL, c_consts)] + G(BK[bi])) if j == 0 else [],
                                    inc=tE if j == 3 else None)
                    ev = p.emit("vector", lambda h, half=half, bank=bank:
                                h.tensor_copy(out=kst[half % 2].rearrange("p (c n) -> p c n", n=128),
                                              in_=bank),
                                waits=[(tE, tw)] + G(f"kst{half % 2}") + G("prepm"), inc=vE)
                    setg(BK[bi], vE, ev)
                    sp = p.emit("sync", lambda h, t=t, half=half: h.dma_start(
                        out=wfb3[:, half * 4:(half + 1) * 4, t * 128:(t + 1) * 128],
                        in_=kst[half % 2].rearrange("p (c n) -> p c n", n=128)),
                                waits=[(vE, ev)], inc=dO, inck=16)
                    setg(f"kst{half % 2}", dO, sp)
                    last_sp = sp
                setg("xnt0w", vE, vE.n)
                m1 = p.emit("tensor", lambda h, t=t: h.matmul(out=pbank[6][0:1, 0:128], lhsT=wsclT[:, t:t + 1],
                                                              rhs=idfb, start=True, stop=True),
                            waits=[(vE, vrec), (dL, c_consts)] + G(BK[6]), inc=tE)
                e1 = p.emit("vector", lambda h, t=t: h.tensor_copy(out=wsrow[0:1, t * 128:(t + 1) * 128],
                                                                   in_=pbank[6][0:1, 0:128]),
                            waits=[(tE, m1)], inc=vE)
                setg(BK[6], vE, e1)
                m2 = p.emit("tensor", lambda h, t=t: h.matmul(out=pbank[7][:, 0:128], lhsT=onesr,
                                                              rhs=wsrow[0:1, t * 128:(t + 1) * 128],
                                                              start=True, stop=True),
                            waits=[(vE, e1)] + G(BK[7]), inc=tE)
                e2 = p.emit("vector", lambda h, t=t: h.tensor_copy(out=am32[:, t * 128:(t + 1) * 128],
                                                                   in_=pbank[7][:, 0:128]),
                            waits=[(tE, m2)] + G("prepm"), inc=vE)
                setg(BK[7], vE, e2)
            prepv['wf_spill_done'] = last_sp
            wscl_spill = p.emit("sync", lambda h: h.dma_start(out=wscl, in_=am32[:, 0:2048]),
                                waits=[(vE, vE.n)], inc=dO, inck=16)
            prepv['wscl_spill'] = wscl_spill
            setg("wsrow_rd", dO, wscl_spill)

        spill_record = {}

        for s in range(nstripe):
            tok0 = s * STRIPE
            first = (s == 0)
            seq_start = (s % 2 == 0)

            xload = [None] * NQT

            def emit_xload(qt, src_t, extra=(), nb=2):
                idx = qt % nb
                xload[qt] = p.emit("gpsimd", lambda h, qt=qt, src_t=src_t, tok0=tok0, idx=idx: h.dma_start(
                    out=xstage[idx], in_=src_t[tok0 + qt * 128: tok0 + (qt + 1) * 128, :]),
                           waits=G(f"xst{idx}") + list(extra), inc=dG, inck=16)
                guard.pop(f"xst{idx}", None)

            emit_xload(0, x)
            emit_xload(1, x)

            def load_weight(dst_fn, n_tiles, width, fsrc, bfscr, tag, extra=()):
                done = []
                if first:
                    spills = []
                    for t in range(n_tiles):
                        w = p.emit("sync", lambda h, t=t, width=width, fsrc=fsrc: h.dma_start(
                            out=wstage[t % 2][:, 0:width], in_=fsrc[t * 128:(t + 1) * 128, :]),
                                   waits=G(f"wst{t % 2}"), inc=dL, inck=16)
                        if t % 2 == 0:
                            c = p.emit("vector", lambda h, t=t, width=width: h.tensor_copy(
                                out=dst_fn(t), in_=wstage[t % 2][:, 0:width]),
                                       waits=[(dL, w)] + G(f"{tag}{t}") + list(extra), inc=vE)
                            csem = vE
                        else:
                            c = p.emit("scalar", lambda h, t=t, width=width: h.activation(
                                out=dst_fn(t), in_=wstage[t % 2][:, 0:width], func=AF.Identity, scale=1.0),
                                       waits=[(dL, w)] + G(f"{tag}{t}") + list(extra), inc=sE)
                            csem = sE
                        setg(f"wst{t % 2}", csem, c)
                        sp = p.emit("sync", lambda h, t=t, bfscr=bfscr: h.dma_start(
                            out=bfscr[t * 128:(t + 1) * 128, :], in_=dst_fn(t)),
                                    waits=[(csem, c)], inc=dO, inck=16)
                        spills.append(sp)
                        done.append((csem, c))
                    spill_record[tag] = spills[-1]
                else:
                    for t in range(n_tiles):
                        w = p.emit("sync", lambda h, t=t, bfscr=bfscr: h.dma_start(
                            out=dst_fn(t), in_=bfscr[t * 128:(t + 1) * 128, :]),
                                   waits=G(f"{tag}{t}") + [(dO, spill_record[tag])] + list(extra),
                                   inc=dL, inck=16)
                        done.append((dL, w))
                return done

            wk_tile = lambda t: arena_q[:, t * 512:(t + 1) * 512]
            wv_tile = lambda t: arena_q[:, 8192 + t * 512: 8192 + (t + 1) * 512]
            w_tile = lambda t: arena_w[:, t * 2048:(t + 1) * 2048]

            ln1_done = [None] * NQT
            ln1_s3 = {}

            def ln1_front(qt):
                c0 = qt % 2
                base = c0 * 4
                w = xload[qt]
                v1 = p.emit("vector", lambda h, qt=qt, base=base: h.reduce_sum(
                    out=statc[:, base:base + 1], in_=xstage[qt % 2], axis=AX.X),
                            waits=[(dG, w)] + G(f"mcol{base}"), inc=vE)
                sq = None
                for j in range(4):
                    sq = p.emit("scalar", lambda h, j=j, qt=qt: h.activation(
                        out=sqs, in_=xstage[qt % 2][:, j * 512:(j + 1) * 512], func=AF.Square, scale=1.0,
                        accum_out=statc[:, 8 + j:9 + j]),
                        waits=[(dG, w)] + (G("sqacc") if j == 0 else []), inc=sE)
                v2 = p.emit("vector", lambda h, base=base: h.reduce_sum(
                    out=statc[:, base + 1:base + 2], in_=statc[:, 8:12], axis=AX.X),
                            waits=[(sE, sq)] + G(f"var{c0}"), inc=vE)
                setg("sqacc", vE, v2)
                v3 = p.emit("vector", lambda h, base=base: h.tensor_scalar_mul(
                    out=statc[:, base:base + 1], in0=statc[:, base:base + 1], scalar1=INVD), inc=vE)
                v4 = p.emit("vector", lambda h, base=base: h.tensor_scalar_mul(
                    out=statc[:, base + 1:base + 2], in0=statc[:, base + 1:base + 2], scalar1=INVD), inc=vE)
                v5 = p.emit("vector", lambda h, base=base: h.tensor_tensor(
                    out=statc[:, 12:13], in0=statc[:, base:base + 1], in1=statc[:, base:base + 1],
                    op=ALU.mult),
                            waits=G("col12") + G("col12r"), inc=vE)
                p.emit("vector", lambda h: h.tensor_copy(out=jnk, in_=epsc))
                v6 = p.emit("vector", lambda h, base=base: h.scalar_tensor_tensor(
                    out=statc[:, base + 1:base + 2], in0=statc[:, 12:13], scalar=-1.0,
                    in1=statc[:, base + 1:base + 2], op0=ALU.mult, op1=ALU.add), inc=vE)
                s1 = p.emit("scalar", lambda h, base=base: h.activation(
                    out=statc[:, 12:13], in_=statc[:, base + 1:base + 2], func=AF.Ln, bias=epsc, scale=1.0),
                            waits=[(vE, v6)], inc=sE)
                setg(f"var{c0}", sE, s1)
                setg("col12", sE, s1)
                s2 = p.emit("scalar", lambda h, base=base: h.activation(
                    out=statc[:, base + 2:base + 3], in_=statc[:, 12:13], func=AF.Exp, scale=-0.5),
                            waits=G(f"rs{c0}"), inc=sE)
                setg("col12r", sE, s2)
                v7 = p.emit("vector", lambda h, base=base: h.scalar_tensor_tensor(
                    out=statc[:, base + 3:base + 4], in0=statc[:, base:base + 1], scalar=-1.0,
                    in1=statc[:, base + 2:base + 3], op0=ALU.mult, op1=ALU.mult),
                            waits=[(sE, s2)] + G(f"nb{c0}"), inc=vE)
                s3 = p.emit("scalar", lambda h, qt=qt, base=base: h.activation(
                    out=xnt[qt % 2], in_=xstage[qt % 2], func=AF.Identity,
                    scale=statc[:, base + 2:base + 3], bias=statc[:, base + 3:base + 4]),
                            waits=[(vE, v7)] + G(f"xnt{qt % 2}"), inc=sE)
                setg(f"rs{c0}", sE, s3)
                setg(f"nb{c0}", sE, s3)
                setg(f"mcol{base}", sE, s3)
                setg(f"xst{qt % 2}", sE, s3)
                if qt + 2 < NQT:
                    emit_xload(qt + 2, x)
                ln1_s3[qt] = s3

            def ln1_back(qt):
                s3 = ln1_s3.pop(qt)
                lastev = None
                for half in range(4):
                    bi = 4 + half % 2
                    bank = pbank[bi]
                    tw = None
                    for j in range(4):
                        dd = half * 4 + j
                        tw = p.emit("tensor", lambda h, dd=dd, j=j, bank=bank, qt=qt:
                                    h.matmul(out=bank[:, j * 128:(j + 1) * 128],
                                             lhsT=xnt[qt % 2][:, dd * 128:(dd + 1) * 128], rhs=idbuf,
                                             start=True, stop=True),
                                    waits=([(sE, s3), (dL, c_consts)] + G(BK[bi])) if j == 0 else [],
                                    inc=tE if j == 3 else None)
                    ev = p.emit("vector", lambda h, half=half, bank=bank, qt=qt:
                                h.tensor_copy(out=am3[:, half * 4:(half + 1) * 4, qt * 128:(qt + 1) * 128],
                                              in_=bank),
                                waits=[(tE, tw)] + (G("misc_ln") if (qt == 0 and half == 0) else []), inc=vE)
                    setg(BK[bi], vE, ev)
                    lastev = ev
                setg(f"xnt{qt % 2}", vE, lastev)
                ln1_done[qt] = lastev

            ln1_front(0)
            for qt in range(NQT):
                if qt + 1 < NQT:
                    ln1_front(qt + 1)
                ln1_back(qt)
            xnf_all = max(ln1_done)

            wk_done = load_weight(wk_tile, 16, 512, wkT, wkbf, "tk")
            wv_done = load_weight(wv_tile, 16, 512, wvT, wvbf, "tv")
            wq_done = load_weight(w_tile, 16, 2048, wqT, wqbf, "tq")

            if maxphase < 1:
                continue
            tail_done = 0
            if not seq_start:
                for c in range(NKV):
                    p.emit("vector", lambda h, c=c: h.tensor_copy(
                        out=kf[:, c * KCOLS: c * KCOLS + 256], in_=kf[:, c * KCOLS + 1024: c * KCOLS + 1280]),
                           waits=G("kf_rw"), inc=vE)
                    tail_done = p.emit("vector", lambda h, c=c: h.tensor_copy(
                        out=vt[:, c * KCOLS: c * KCOLS + 256], in_=vt[:, c * KCOLS + 1024: c * KCOLS + 1280]),
                           waits=G("vt_rw"), inc=vE)

            kwaits = [wk_done[dd] for dd in range(DK)] + [(vE, xnf_all)]
            kv_ready = 0
            for c in range(NKV):
                for st2 in range(2):
                    bi = (c * 2 + st2) % 2
                    bank = pbank[bi]
                    mm = None
                    for dd in range(DK):
                        mm = p.emit("tensor", lambda h, dd=dd, c=c, st2=st2, bank=bank:
                                    h.matmul(out=bank, lhsT=wk_tile(dd)[:, c * 128:(c + 1) * 128],
                                             rhs=xn_f(dd)[:, st2 * 512:(st2 + 1) * 512],
                                             start=(dd == 0), stop=(dd == DK - 1)),
                                    waits=(kwaits + G(BK[bi])) if dd == 0 else [],
                                    inc=tE if dd == DK - 1 else None)
                    ev = p.emit("vector", lambda h, c=c, st2=st2, bank=bank: h.tensor_copy(
                        out=kf[:, c * KCOLS + 256 + st2 * 512: c * KCOLS + 256 + (st2 + 1) * 512], in_=bank),
                                waits=[(tE, mm), (vE, tail_done)] + G("kf_rw"), inc=vE)
                    setg(BK[bi], vE, ev)
                    kv_ready = ev
                if seq_start:
                    p.emit("vector", lambda h: h.tensor_copy(out=jnk, in_=epsc))
                    kv_ready = p.emit("vector", lambda h, c=c: h.tensor_copy(
                        out=ksink[:, c * 4:(c + 1) * 4], in_=kf[:, c * KCOLS + 256: c * KCOLS + 260]),
                        waits=G("kf_rw"), inc=vE)

            vt3 = vt.rearrange("p (c n) -> p c n", n=KCOLS)
            vwaits = [wv_done[dd] for dd in range(DK)] + [(vE, xnf_all)]
            for qt in range(NQT):
                bi = 2 + qt % 2
                bank = pbank[bi]
                mm = None
                for dd in range(DK):
                    mm = p.emit("tensor", lambda h, dd=dd, qt=qt, bank=bank:
                                h.matmul(out=bank, lhsT=xn_f(dd)[:, qt * 128:(qt + 1) * 128],
                                         rhs=wv_tile(dd),
                                         start=(dd == 0), stop=(dd == DK - 1)),
                                waits=(vwaits + G(BK[bi])) if dd == 0 else [],
                                inc=tE if dd == DK - 1 else None)
                ev = p.emit("vector", lambda h, qt=qt, bank=bank: h.tensor_copy(
                    out=vt3[:, :, (2 + qt) * 128:(3 + qt) * 128], in_=bank),
                            waits=[(tE, mm), (vE, tail_done)] + G("vt_rw"), inc=vE)
                setg(BK[bi], vE, ev)
                if seq_start and qt == 0:
                    p.emit("vector", lambda h: h.tensor_copy(out=jnk, in_=epsc))
                    p.emit("vector", lambda h: h.tensor_copy(
                        out=vsink.rearrange("p (c n) -> p c n", n=128)[0:4, :, :],
                        in_=vt3[0:4, :, 256:384]),
                           waits=[(vE, ev)] + G("vt_rw"), inc=vE)
            v_all = vE.n
            kv_ready = max(kv_ready, v_all)

            if maxphase < 2:
                continue
            qwaits = [wq_done[dd] for dd in range(DK)] + [(vE, xnf_all)]
            kv_mm_end = tE.n
            qf_ready = 0
            for hh in range(NH):
                for st2 in range(2):
                    bi = (hh * 2 + st2) % 2
                    bank = pbank[bi]
                    mm = None
                    for dd in range(DK):
                        mm = p.emit("tensor", lambda h, dd=dd, hh=hh, st2=st2, bank=bank:
                                    h.matmul(out=bank, lhsT=w_tile(dd)[:, hh * 128:(hh + 1) * 128],
                                             rhs=xn_f(dd)[:, st2 * 512:(st2 + 1) * 512],
                                             start=(dd == 0), stop=(dd == DK - 1)),
                                    waits=(qwaits + G(BK[bi])) if dd == 0 else [],
                                    inc=tE if dd == DK - 1 else None)
                    qf_ready = p.emit("vector", lambda h, hh=hh, st2=st2, bank=bank: h.tensor_scalar_mul(
                        out=arena_q[:, hh * 1024 + st2 * 512: hh * 1024 + (st2 + 1) * 512],
                        in0=bank, scalar1=SCALE),
                                waits=[(tE, mm), (tE, kv_mm_end),
                                       (dO, spill_record.get("tk", 0)), (dO, spill_record.get("tv", 0))]
                                      + G(BK[bi] + "x"), inc=vE)
                    setg(BK[bi], vE, qf_ready)
            q_mm_end = tE.n
            if first:
                setg("xnt0w", vE, xnf_all)
                setg("prepm", tE, q_mm_end)
                emit_ffn_prep()
                setg("vt_rw", tE, tE.n)

            if maxphase < 3:
                continue
            for t in range(16):
                setg(f"tq{t}", tE, q_mm_end)
            wo_done = load_weight(w_tile, 16, 2048, woT, wobf, "tq",
                                  extra=[(dO, spill_record.get("tq", 0))])

            xrel = xload
            emit_xload(0, x)
            emit_xload(1, x)

            misc_attn = (tE, q_mm_end)
            if first:
                misc_attn = (dO, prepv['wscl_spill'])
            pend_o = None
            o_state = {}

            def emit_o_slice(oqt, sl, ring_ready):
                bi4 = 6 + sl % 2
                bank5 = pbank[bi4]
                mm = None
                for hh in range(NH):
                    mm = p.emit("tensor", lambda h, hh=hh, sl=sl, bank5=bank5, oqt=oqt:
                                h.matmul(out=bank5, lhsT=ring[oqt % 2][:, hh * 128:(hh + 1) * 128],
                                         rhs=w_tile(hh)[:, sl * 512:(sl + 1) * 512],
                                         start=(hh == 0), stop=(hh == NH - 1)),
                                waits=([(vE, ring_ready)] + [wo_done[t] for t in range(16)]
                                       + G(BK[bi4])) if hh == 0 else [],
                                inc=tE if hh == NH - 1 else None)
                xv = p.emit("vector", lambda h, sl=sl, oqt=oqt, bank5=bank5: h.tensor_tensor(
                    out=xps[oqt % 2][:, sl * 512:(sl + 1) * 512], in0=bank5,
                    in1=xstage[oqt % 2][:, sl * 512:(sl + 1) * 512], op=ALU.add),
                            waits=[(tE, mm), (dG, xrel[oqt])]
                                  + (G(f"xps{oqt % 2}") if sl == 0 else [])
                                  + ([misc_attn] if oqt <= 1 else []),
                            inc=vE)
                setg(BK[bi4], vE, xv)
                o_state[oqt] = xv

            def emit_o_finish(oqt, tok0=tok0):
                xv_last = o_state.pop(oqt)
                setg(f"ring{oqt % 2}", tE, tE.n)
                setg(f"xst{oqt % 2}", vE, xv_last)
                if oqt + 2 < NQT:
                    emit_xload(oqt + 2, x)
                sp = p.emit("sync", lambda h, oqt=oqt, tok0=tok0: h.dma_start(
                    out=xp[tok0 + oqt * 128: tok0 + (oqt + 1) * 128, :], in_=xps[oqt % 2]),
                            waits=[(vE, xv_last)], inc=dO, inck=16)
                setg(f"xps{oqt % 2}", dO, sp)

            for qt in range(NQT):
                g = (s % 2) * NQT + qt
                if g == 0:
                    W, nch, has_sink, moff = 128, 1, False, 0
                elif g == 1:
                    W, nch, has_sink, moff = 256, 2, False, 128
                else:
                    W, nch, has_sink, moff = 384, 3, True, (384 if g == 2 else 772)
                kstart = 256 if g <= 1 else qt * 128
                wtot = W + (4 if has_sink else 0)
                nall = nch + (1 if has_sink else 0)

                st = {}
                pend_pts = []
                group_ev = {}

                def emit_pt(u, qt=qt, nch=nch, has_sink=has_sink, nall=nall):
                    kv, j = u // 4, u % 4
                    dg = st[u]
                    bi2 = 2 + u % 2
                    bank2 = pbank[bi2]
                    for cidx in range(nch):
                        p.emit("tensor", lambda h, cidx=cidx, u=u, bank2=bank2:
                               h.matmul(out=bank2[:, cidx * 128:(cidx + 1) * 128],
                                        lhsT=exb[u % 4][:, cidx * 128:(cidx + 1) * 128],
                                        rhs=diag[u % 4], start=True, stop=True),
                               waits=([(gE, dg)] + G(BK[bi2])) if cidx == 0 else [],
                               inc=tE if (cidx == nch - 1 and not has_sink) else None)
                    if has_sink:
                        p.emit("tensor", lambda h, u=u, bank2=bank2:
                               h.matmul(out=bank2[0:4, 384:512],
                                        lhsT=exb[u % 4][:, 384:388], rhs=diag[u % 4],
                                        start=True, stop=True),
                               inc=tE)
                    ptv = tE.n
                    setg(f"exb{u % 4}", tE, ptv)
                    setg(f"diag{u % 4}", tE, ptv)
                    pt3 = ptbuf[kv % 2].rearrange("p (c n) -> p c n", n=512)
                    bank3 = bank2.rearrange("p (c n) -> p c n", n=128)
                    pe = p.emit("scalar", lambda h, j=j, pt3=pt3, bank3=bank3, nall=nall:
                                h.activation(out=pt3[:, 0:nall, j * 128:(j + 1) * 128],
                                             in_=bank3[:, 0:nall, :], func=AF.Identity, scale=1.0),
                                waits=[(tE, ptv)]
                                      + (G(f"ptbuf{kv % 2}") if j == 0 else [])
                                      + ([misc_attn] if (qt == 0 and kv < 2) else []),
                                inc=sE)
                    setg(BK[bi2], sE, pe)
                    st[u] = pe

                def emit_av(kv, qt=qt, g=g, nch=nch, has_sink=has_sink):
                    bi3 = 4
                    bank4 = pbank[bi3]
                    evs = max(st[kv * 4 + j] for j in range(4))
                    for cidx in range(nch):
                        blk = (2 + cidx) if g <= 1 else (qt + cidx)
                        p.emit("tensor", lambda h, cidx=cidx, kv=kv, blk=blk, bank4=bank4:
                               h.matmul(out=bank4,
                                        lhsT=vt[:, kv * KCOLS + blk * 128: kv * KCOLS + (blk + 1) * 128],
                                        rhs=ptbuf[kv % 2][:, cidx * 512:(cidx + 1) * 512],
                                        start=(cidx == 0), stop=(cidx == nch - 1 and not has_sink)),
                               waits=([(sE, evs), (vE, v_all)] + G(BK[bi3])) if cidx == 0 else [],
                               inc=tE if (cidx == nch - 1 and not has_sink) else None)
                    if has_sink:
                        p.emit("tensor", lambda h, kv=kv, bank4=bank4:
                               h.matmul(out=bank4, lhsT=vsink[0:4, kv * 128:(kv + 1) * 128],
                                        rhs=ptbuf[kv % 2][0:4, 1536:2048],
                                        start=False, stop=True),
                               inc=tE)
                    av_t = tE.n
                    setg(f"ptbuf{kv % 2}", tE, av_t)
                    rv = p.emit("vector", lambda h, kv=kv, qt=qt, bank4=bank4: h.tensor_copy(
                        out=ring[qt % 2][:, kv * 512:(kv + 1) * 512], in_=bank4),
                                waits=[(tE, av_t)] + (G(f"ring{qt % 2}") if kv == 0 else [])
                                      + ([misc_attn] if qt == 0 else []),
                                inc=vE)
                    setg(BK[bi3], vE, rv)
                    group_ev[kv] = rv

                for kv in range(NKV):
                    for j in range(4):
                        u = kv * 4 + j
                        hh = kv * 4 + j
                        bi = [0, 1, 5][u % 3]
                        bank = pbank[bi]
                        mm = p.emit("tensor", lambda h, hh=hh, kv=kv, bank=bank, W=W, kstart=kstart, qt=qt:
                                    h.matmul(out=bank[:, 0:W],
                                             lhsT=arena_q[:, hh * 1024 + qt * 128: hh * 1024 + qt * 128 + 128],
                                             rhs=kf[:, kv * KCOLS + kstart: kv * KCOLS + kstart + W],
                                             start=True, stop=False),
                                    waits=[(vE, qf_ready), (vE, kv_ready)] + G(BK[bi]))
                        if has_sink:
                            mm = p.emit("tensor", lambda h, hh=hh, kv=kv, bank=bank, qt=qt:
                                        h.matmul(out=bank[:, 384:388],
                                                 lhsT=arena_q[:, hh * 1024 + qt * 128: hh * 1024 + qt * 128 + 128],
                                                 rhs=ksink[:, kv * 4:(kv + 1) * 4],
                                                 start=False, stop=False))
                        mm = p.emit("tensor", lambda h, bank=bank, wtot=wtot, moff=moff:
                                    h.matmul(out=bank[:, 0:wtot], lhsT=idbuf,
                                             rhs=maskb[:, moff:moff + wtot],
                                             start=False, stop=True),
                                    waits=[(dL, c_consts)], inc=tE)
                        sc = tE.n
                        mx = p.emit("vector", lambda h, u=u, bank=bank, wtot=wtot: h.reduce_max(
                            out=negm[:, u % 4:u % 4 + 1], in_=bank[:, 0:wtot], axis=AX.X, negate=True),
                                    waits=[(tE, sc)] + G(f"negm{u % 4}"), inc=vE)
                        ex_ = p.emit("scalar", lambda h, u=u, bank=bank, wtot=wtot: h.activation(
                            out=exb[u % 4][:, 0:wtot], in_=bank[:, 0:wtot], func=AF.Exp,
                            bias=negm[:, u % 4:u % 4 + 1], scale=1.0,
                            accum_out=den[:, u % 4:u % 4 + 1]),
                                     waits=[(vE, mx)] + G(f"exb{u % 4}") + G(f"den{u % 4}"), inc=sE)
                        setg(BK[bi], sE, ex_)
                        setg(f"negm{u % 4}", sE, ex_)
                        def _recip(h, u=u):
                            with nc.allow_low_precision(reason="alpha in bf16 feeds bf16 matmul"):
                                return h.reciprocal(out=alphab[:, u % 4:u % 4 + 1],
                                                    in_=den[:, u % 4:u % 4 + 1])
                        rc = p.emit("vector", _recip, waits=[(sE, ex_)] + G(f"alf{u % 4}"), inc=vE)
                        setg(f"den{u % 4}", vE, rc)
                        dg = p.emit("gpsimd", lambda h, u=u: h.tensor_tensor(
                            out=diag[u % 4], in0=alphab[:, u % 4:u % 4 + 1].broadcast_to((128, 128)),
                            in1=idbuf, op=ALU.mult),
                                    waits=[(vE, rc), (dL, c_consts)] + G(f"diag{u % 4}"), inc=gE)
                        setg(f"alf{u % 4}", gE, dg)
                        st[u] = dg
                        pend_pts.append(u)
                        if len(pend_pts) > 3:
                            pu = pend_pts.pop(0)
                            emit_pt(pu)
                            if pu % 4 == 3:
                                emit_av(pu // 4)
                                if pend_o is not None:
                                    oqt, orr = pend_o
                                    emit_o_slice(oqt, pu // 4, orr)
                                    if pu // 4 == 3:
                                        emit_o_finish(oqt)
                                        pend_o = None
                for pu in pend_pts:
                    emit_pt(pu)
                    if pu % 4 == 3:
                        emit_av(pu // 4)
                        if pend_o is not None:
                            oqt, orr = pend_o
                            emit_o_slice(oqt, pu // 4, orr)
                            if pu // 4 == 3:
                                emit_o_finish(oqt)
                                pend_o = None
                pend_pts = []
                pend_o = (qt, max(group_ev.values()))
            oqt, orr = pend_o
            for sl in range(4):
                emit_o_slice(oqt, sl, orr)
            emit_o_finish(oqt)
            pend_o = None
            setg(f"xpsl{0}", dO, guard[f"xps{0}"][1])
            setg(f"xpsl{1}", dO, guard[f"xps{1}"][1])
            attn_te_end = tE.n

            if maxphase < 4:
                continue
            for t in range(16):
                setg(f"tq{t}", tE, attn_te_end)
            wf_done = []
            for t in range(16):
                w = p.emit("sync", lambda h, t=t: h.dma_start(out=w_tile(t), in_=wfbf[t * 128:(t + 1) * 128, :]),
                           waits=G(f"tq{t}") + [(dO, prepv['wf_spill_done']), (dO, spill_record.get("tq", 0))],
                           inc=dL, inck=16)
                wf_done.append((dL, w))
            setg("outst", dO, max(guard.get("xps0", (None, 0))[1], guard.get("xps1", (None, 0))[1]))
            wslw = p.emit("sync", lambda h: h.dma_start(out=wsb, in_=wscl),
                          waits=[(tE, attn_te_end), (dO, prepv['wscl_spill'])], inc=dL, inck=16)
            bblw = p.emit("sync", lambda h: h.dma_start(out=bib, in_=biasb),
                          waits=[(tE, attn_te_end)], inc=dL, inck=16)

            out_dma_last = 0
            emit_xload(0, xp, extra=G(f"xpsl0"), nb=3)
            emit_xload(1, xp, extra=G(f"xpsl1"), nb=3)
            emit_xload(2, xp, extra=G(f"xpsl0") + list(wo_done[-2:]), nb=3)
            ffn_lastev = {}
            ffn_sq2 = {}

            def ffn_stats(qt):
                pb = (qt % 3) * 4
                xw = xload[qt]
                v1 = p.emit("vector", lambda h, qt=qt, pb=pb: h.reduce_sum(
                    out=statc[:, pb + 3:pb + 4], in_=xstage[qt % 3], axis=AX.X),
                            waits=[(dG, xw)] + G(f"mf{qt % 3}"), inc=vE)
                sq = None
                for j in range(4):
                    sq = p.emit("scalar", lambda h, j=j, qt=qt: h.activation(
                        out=sqs, in_=xstage[qt % 3][:, j * 512:(j + 1) * 512], func=AF.Square, scale=1.0,
                        accum_out=statc[:, 16 + j:17 + j]),
                        waits=[(dG, xw)] + (G("sqaccF") if j == 0 else []), inc=sE)
                vmx = p.emit("vector", lambda h, qt=qt: h.reduce_max(
                    out=statc[:, 13:14], in_=xstage[qt % 3], axis=AX.X),
                             waits=[(dG, xw)] + G("col13"), inc=vE)
                vmn = p.emit("vector", lambda h, qt=qt: h.tensor_reduce(
                    out=statc[:, 14:15], in_=xstage[qt % 3], axis=AX.X, op=ALU.min),
                             waits=[(dG, xw)], inc=vE)
                v2 = p.emit("vector", lambda h: h.reduce_sum(out=statc[:, 15:16], in_=statc[:, 16:20], axis=AX.X),
                            waits=[(sE, sq)], inc=vE)
                setg("sqaccF", vE, v2)
                v3 = p.emit("vector", lambda h, pb=pb: h.tensor_scalar_mul(
                    out=statc[:, pb + 3:pb + 4], in0=statc[:, pb + 3:pb + 4], scalar1=INVD), inc=vE)
                v4 = p.emit("vector", lambda h: h.tensor_scalar_mul(
                    out=statc[:, 15:16], in0=statc[:, 15:16], scalar1=INVD), inc=vE)
                v5 = p.emit("vector", lambda h, pb=pb: h.tensor_tensor(
                    out=statc[:, 12:13], in0=statc[:, pb + 3:pb + 4], in1=statc[:, pb + 3:pb + 4],
                    op=ALU.mult),
                            waits=G("col12"), inc=vE)
                v7 = p.emit("vector", lambda h, pb=pb: h.tensor_tensor(
                    out=statc[:, 13:14], in0=statc[:, 13:14], in1=statc[:, pb + 3:pb + 4],
                    op=ALU.subtract),
                            waits=[(vE, vmx)], inc=vE)
                v6 = p.emit("vector", lambda h: h.scalar_tensor_tensor(
                    out=statc[:, 12:13], in0=statc[:, 12:13], scalar=-1.0, in1=statc[:, 15:16],
                    op0=ALU.mult, op1=ALU.add), inc=vE)
                v8 = p.emit("vector", lambda h, pb=pb: h.tensor_tensor(
                    out=statc[:, 14:15], in0=statc[:, pb + 3:pb + 4], in1=statc[:, 14:15],
                    op=ALU.subtract), inc=vE)
                s1 = p.emit("scalar", lambda h: h.activation(
                    out=statc[:, 12:13], in_=statc[:, 12:13], func=AF.Ln, bias=epsc, scale=1.0),
                            waits=[(vE, v6)], inc=sE)
                setg("col12", sE, s1)
                s2 = p.emit("scalar", lambda h: h.activation(
                    out=statc[:, 20:21], in_=statc[:, 12:13], func=AF.Exp, scale=-0.5),
                            waits=G("rs20"), inc=sE)
                p.emit("vector", lambda h: h.tensor_copy(out=jnk, in_=epsc))
                v9 = p.emit("vector", lambda h: h.tensor_tensor(
                    out=statc[:, 13:14], in0=statc[:, 13:14], in1=statc[:, 14:15], op=ALU.max), inc=vE)
                p.emit("vector", lambda h: h.tensor_copy(out=jnk, in_=epsc))
                v10 = p.emit("vector", lambda h: h.tensor_tensor(
                    out=statc[:, 13:14], in0=statc[:, 13:14], in1=statc[:, 20:21], op=ALU.mult),
                             waits=[(sE, s2)], inc=vE)
                p.emit("vector", lambda h: h.tensor_copy(out=jnk, in_=epsc))
                v11 = p.emit("vector", lambda h, pb=pb: h.tensor_scalar(
                    out=statc[:, pb:pb + 1], in0=statc[:, 13:14], scalar1=INV127, scalar2=1e-12,
                    op0=ALU.mult, op1=ALU.max), inc=vE)
                v12 = p.emit("vector", lambda h: h.tensor_copy(out=jnk, in_=epsc), inc=vE)
                v13 = p.emit("vector", lambda h, pb=pb: h.reciprocal(
                    out=statc[:, pb + 1:pb + 2], in_=statc[:, pb:pb + 1]),
                             waits=G(f"sbq{qt % 3}"), inc=vE)
                p.emit("vector", lambda h: h.tensor_copy(out=jnk, in_=epsc))
                v14 = p.emit("vector", lambda h, pb=pb: h.tensor_tensor(
                    out=statc[:, pb + 1:pb + 2], in0=statc[:, pb + 1:pb + 2],
                    in1=statc[:, 20:21], op=ALU.mult), inc=vE)
                setg("col13", vE, v14)
                setg("rs20", vE, v14)
                p.emit("vector", lambda h: h.tensor_copy(out=jnk, in_=epsc))
                v15 = p.emit("vector", lambda h, pb=pb: h.scalar_tensor_tensor(
                    out=statc[:, pb + 2:pb + 3], in0=statc[:, pb + 3:pb + 4], scalar=-1.0,
                    in1=statc[:, pb + 1:pb + 2], op0=ALU.mult, op1=ALU.mult), inc=vE)
                sq2l = []
                sq1 = sq2 = None
                for hh2 in range(4):
                    sq1 = p.emit("scalar", lambda h, qt=qt, pb=pb, hh2=hh2: h.activation(
                        out=xq8[:, hh2 * 512:(hh2 + 1) * 512],
                        in_=xstage[qt % 3][:, hh2 * 512:(hh2 + 1) * 512], func=AF.Identity,
                        scale=statc[:, pb + 1:pb + 2], bias=statc[:, pb + 2:pb + 3]),
                                 waits=([(vE, v15)] + G("xq8")) if hh2 == 0 else [], inc=sE)
                    sq2 = p.emit("scalar", lambda h, qt=qt, hh2=hh2: h.activation(
                        out=xqtb[qt % 2][:, hh2 * 512:(hh2 + 1) * 512],
                        in_=xq8[:, hh2 * 512:(hh2 + 1) * 512], func=AF.Identity, scale=1.0),
                                 waits=(G(f"xqt{qt % 2}") + G(f"xpsl{1}")
                                        + ([] if qt else (G("misc_ffn") + G("wsrow_rd")))) if hh2 == 0 else [],
                                 inc=sE)
                    sq2l.append(sq2)
                setg(f"sbq{qt % 3}", sE, sq1)
                setg(f"mf{qt % 3}", sE, sq1)
                setg("xq8", sE, sq2)
                ffn_sq2[qt] = sq2l

            def ffn_tr(qt):
                sq2l = ffn_sq2.pop(qt)
                lastev = None
                for half in range(4):
                    bi = 2 + half % 2
                    bank = pbank[bi]
                    tw = None
                    for j in range(4):
                        dd = half * 4 + j
                        tw = p.emit("tensor", lambda h, dd=dd, j=j, bank=bank, qt=qt:
                                    h.matmul(out=bank[:, j * 128:(j + 1) * 128],
                                             lhsT=xqtb[qt % 2][:, dd * 128:(dd + 1) * 128], rhs=idbuf,
                                             start=True, stop=True),
                                    waits=([(sE, sq2l[half])] + G(BK[bi])) if j == 0 else [],
                                    inc=tE if j == 3 else None)
                    aq3 = arena_q.rearrange("p (c n) -> p c n", n=1024)
                    ev = p.emit("vector", lambda h, half=half, bank=bank, qt=qt, aq3=aq3:
                                h.tensor_copy(out=aq3[:, half * 4:(half + 1) * 4, qt * 128:(qt + 1) * 128],
                                              in_=bank),
                                waits=[(tE, tw), (tE, attn_te_end),
                                       (dO, spill_record.get("tv", 0))], inc=vE)
                    setg(BK[bi], vE, ev)
                    lastev = ev
                setg(f"xqt{qt % 2}", vE, lastev)
                ffn_lastev[qt] = lastev

            def ffn_back(qt, mid=None):
                base = (qt % 3) * 4
                lastev = ffn_lastev.pop(qt)
                mmv = [None] * 4
                e1v = [None] * 4

                def ffn_mm(sl, qt=qt, lastev=lastev):
                    bi4 = 6 + sl % 2
                    bank5 = pbank[bi4]
                    mm = None
                    for dd in range(DK):
                        mm = p.emit("tensor", lambda h, dd=dd, sl=sl, bank5=bank5, qt=qt:
                                    h.matmul(out=bank5,
                                             lhsT=arena_q[:, dd * 1024 + qt * 128: dd * 1024 + qt * 128 + 128],
                                             rhs=w_tile(dd)[:, sl * 512:(sl + 1) * 512],
                                             start=(dd == 0), stop=(dd == DK - 1)),
                                    waits=([(vE, lastev), (dL, wslw), (dL, bblw)]
                                           + [wf_done[t] for t in range(16)]
                                           + G(BK[bi4])) if dd == 0 else [],
                                    inc=tE if dd == DK - 1 else None)
                    mmv[sl] = mm

                def ffn_e1(sl, base=base):
                    bi4 = 6 + sl % 2
                    e1v[sl] = p.emit("vector", lambda h, sl=sl, base=base: h.scalar_tensor_tensor(
                        out=outst[:, sl * 512:(sl + 1) * 512], in0=pbank[6 + sl % 2],
                        scalar=statc[:, base:base + 1], in1=wsb[:, sl * 512:(sl + 1) * 512],
                        op0=ALU.mult, op1=ALU.mult),
                                waits=[(tE, mmv[sl]), (dL, wslw)]
                                      + (G("outst") if sl == 0 else []), inc=vE)
                    setg(BK[bi4], vE, e1v[sl])

                ffn_mm(0); ffn_mm(1)
                ffn_e1(0)
                if mid is not None:
                    mid()
                ffn_mm(2)
                ffn_e1(1)
                ffn_mm(3)
                ffn_e1(2); ffn_e1(3)
                for sl in range(4):
                    p.emit("vector", lambda h, sl=sl: h.tensor_tensor(
                        out=tmpv[sl], in0=outst[:, sl * 512:(sl + 1) * 512],
                        in1=bib[:, sl * 512:(sl + 1) * 512], op=ALU.add),
                           waits=[(dL, bblw)] + (G("tmpv") if sl == 0 else []), inc=vE)
                out_writes = 0
                for sl in range(4):
                    out_writes = p.emit("vector", lambda h, sl=sl, qt=qt: h.tensor_tensor(
                        out=outst[:, sl * 512:(sl + 1) * 512], in0=tmpv[sl],
                        in1=xstage[qt % 3][:, sl * 512:(sl + 1) * 512], op=ALU.add), inc=vE)
                setg("tmpv", vE, out_writes)
                setg(f"xst{qt % 3}", vE, out_writes)
                if qt + 3 < NQT:
                    emit_xload(qt + 3, xp, extra=G(f"xpsl{(qt + 3) % 2}"), nb=3)
                ow = p.emit("sync", lambda h, qt=qt, tok0=tok0: h.dma_start(
                    out=y[tok0 + qt * 128: tok0 + (qt + 1) * 128, :], in_=outst),
                            waits=[(vE, out_writes)], inc=dO, inck=16)
                setg("outst", dO, ow)
                return ow

            ffn_stats(0)
            ffn_stats(1)
            ffn_tr(0)
            for qt in range(NQT):
                if qt + 2 < NQT:
                    ffn_stats(qt + 2)
                if qt + 1 < NQT:
                    out_dma_last = ffn_back(qt, mid=lambda qt=qt: ffn_tr(qt + 1))
                else:
                    out_dma_last = ffn_back(qt)
            ffn_te_end = tE.n
            setg("kf_rw", tE, attn_te_end)
            setg("vt_rw", tE, attn_te_end)
            setg("misc_ln", dO, out_dma_last)
            setg("xps0", dO, out_dma_last)
            setg("xps1", dO, out_dma_last)
            setg("outst", dO, out_dma_last)
            setg("misc_ffn", dO, guard.get("xps0", (dO, 0))[1])
            for t in range(16):
                setg(f"tq{t}", tE, ffn_te_end)
                setg(f"tk{t}", tE, ffn_te_end)
                setg(f"tv{t}", tE, ffn_te_end)
            setg("awq", tE, ffn_te_end)
            setg(BK[0] + "x", tE, ffn_te_end)
            setg(BK[1] + "x", tE, ffn_te_end)

        if debug:
            for dst, src_ in [(dkf, kf), (dvt, vt), (dq, arena_q), (dm, arena_m)]:
                p.emit("sync", lambda h, d=dst, s_=src_: h.dma_start(out=d, in_=s_),
                       waits=[(dO, dO.n), (tE, tE.n), (vE, vE.n), (sE, sE.n)], inc=dO, inck=16)
            p.emit("sync", lambda h: h.dma_start(out=dxp, in_=xp), inc=dO, inck=16)
            p.emit("sync", lambda h: h.dma_start(out=dstat, in_=statc[:, 0:16]), inc=dO, inck=16)
            p.emit("sync", lambda h: h.dma_start(out=dxs0, in_=xstage[0]), inc=dO, inck=16)
            p.emit("sync", lambda h: h.dma_start(out=dxs1, in_=xstage[1]), inc=dO, inck=16)
            p.emit("sync", lambda h: h.dma_start(out=dxnt1, in_=xnt[1]), inc=dO, inck=16)
            p.emit("sync", lambda h: h.dma_start(out=dwk, in_=wkbf), inc=dO, inck=16)
            p.emit("sync", lambda h: h.dma_start(out=dwo, in_=wobf), inc=dO, inck=16)
            p.emit("sync", lambda h: h.dma_start(out=dwf, in_=wfbf), inc=dO, inck=16)
            p.emit("sync", lambda h: h.dma_start(out=dws, in_=wscl), inc=dO, inck=16)
        final_dO = dO.n

        with nc.Block() as block:
            @block.sync
            def _(sync):
                for f in p.q["sync"]:
                    f(sync)
                sync.wait_ge(dO.h, final_dO)

            @block.tensor
            def _(tensor):
                for f in p.q["tensor"]:
                    f(tensor)

            @block.vector
            def _(vector):
                for f in p.q["vector"]:
                    f(vector)

            @block.scalar
            def _(scalar):
                for f in p.q["scalar"]:
                    f(scalar)

            @block.gpsimd
            def _(gpsimd):
                for f in p.q["gpsimd"]:
                    f(gpsimd)

    return nc


def _build_consts():
    r = np.arange(128)[:, None]
    j1 = np.arange(128)[None, :]
    j2 = np.arange(256)[None, :]
    j3 = np.arange(384)[None, :]
    m0 = (j1 <= r)
    m1 = (j2 <= 128 + r)
    band = (j3 >= r + 1) & (j3 < r + 257)
    s2 = (np.arange(4)[None, :] < np.minimum(4, r + 1))
    sS = np.ones((128, 4), bool)
    m2 = np.concatenate([band, s2], axis=1)
    mS = np.concatenate([band, sS], axis=1)
    full = np.concatenate([m0, m1, m2, mS], axis=1)
    masks = np.where(full, np.float32(0.0), np.float32(-1e30)).astype(ml_dtypes.bfloat16)
    idf = np.eye(128, dtype=np.float32)
    idb = np.eye(128).astype(ml_dtypes.bfloat16)
    return masks, idf, idb


def kernel(L0_attn_x, L0_attn_wq, L0_attn_wk, L0_attn_wv, L0_attn_wo, L0_ffn_weight, L0_ffn_bias):
    x = np.ascontiguousarray(np.asarray(L0_attn_x, dtype=np.float32))
    wq = np.asarray(L0_attn_wq, np.float32)
    wk = np.asarray(L0_attn_wk, np.float32)
    wv = np.asarray(L0_attn_wv, np.float32)
    wo = np.asarray(L0_attn_wo, np.float32)
    wf = np.ascontiguousarray(np.asarray(L0_ffn_weight, np.float32))
    bias = np.asarray(L0_ffn_bias, np.float32)

    wqT = np.ascontiguousarray(wq.T)
    wkT = np.ascontiguousarray(wk.T)
    wvT = np.ascontiguousarray(wv.T)
    woT = np.ascontiguousarray(wo.T)
    biasb = np.ascontiguousarray(np.broadcast_to(bias[None, :], (128, D)))
    masks, idf, idb = _build_consts()

    nc = build_nc()
    in_maps = []
    for i in range(NCORES):
        xi = np.ascontiguousarray(x[2 * i: 2 * i + 2].reshape(TOK, D))
        in_maps.append(dict(x=xi, wqT=wqT, wkT=wkT, wvT=wvT, woT=woT, wfO=wf,
                            biasb=biasb, idf=idf, idb=idb, masks=masks))
    res = run_bass_kernel_spmd(nc, in_maps, core_ids=list(range(NCORES)))
    out = np.empty((B, T, D), np.float32)
    for i in range(NCORES):
        out[2 * i: 2 * i + 2] = res.results[i]["y"].reshape(2, T, D)
    return out


# version 18
# speedup vs baseline: 1.0240x; 1.0041x over previous
import math
import numpy as np
import ml_dtypes
import concourse.bass as bass
import concourse.mybir as mybir
from concourse.bass_utils import run_bass_kernel_spmd
from contextlib import ExitStack

F32 = mybir.dt.float32
BF16 = mybir.dt.bfloat16
I8 = mybir.dt.int8
AF = mybir.ActivationFunctionType
ALU = mybir.AluOpType
AX = mybir.AxisListType

NCORES = 8
B, T, D = 16, 2048, 2048
TOK = (B // NCORES) * T
STRIPE = 1024
NSTRIPE = TOK // STRIPE
NQT = STRIPE // 128
NH, NKV, HD = 16, 4, 128
DK = D // 128
SCALE = 1.0 / math.sqrt(HD)
EPS = 1e-5
INV127 = float(np.float32(1.0 / 127.0))
INVD = 1.0 / 2048.0
KCOLS = 1280


class Sem:
    def __init__(self, stk, nc, name):
        self.h = stk.enter_context(nc.semaphore(name))
        self.n = 0


class Prog:
    ENGS = ("sync", "tensor", "vector", "scalar", "gpsimd")

    def __init__(self):
        self.q = {e: [] for e in self.ENGS}
        self.last_wait = {e: {} for e in self.ENGS}

    def emit(self, eng, fn, waits=(), inc=None, inck=1):
        lw = self.last_wait[eng]
        rw = []
        for sem, val in waits:
            if sem is None or val is None or val <= 0:
                continue
            if lw.get(id(sem), -1) >= val:
                continue
            lw[id(sem)] = val
            rw.append((sem, val))

        def do(h, fn=fn, rw=tuple(rw), inc=inc, inck=inck):
            for s, v in rw:
                h.wait_ge(s.h, v)
            r = fn(h)
            if inc is not None:
                r.then_inc(inc.h, inck)
        self.q[eng].append(do)
        if inc is not None:
            inc.n += inck
            return inc.n
        return None


def build_nc(nstripe=NSTRIPE, debug=False, maxphase=9):
    nc = bass.Bass("TRN2", target_bir_lowering=False)

    x = nc.dram_tensor("x", [TOK, D], F32, kind="ExternalInput").ap()
    wqT = nc.dram_tensor("wqT", [D, D], F32, kind="ExternalInput").ap()
    wkT = nc.dram_tensor("wkT", [D, 512], F32, kind="ExternalInput").ap()
    wvT = nc.dram_tensor("wvT", [D, 512], F32, kind="ExternalInput").ap()
    woT = nc.dram_tensor("woT", [D, D], F32, kind="ExternalInput").ap()
    wfO = nc.dram_tensor("wfO", [D, D], F32, kind="ExternalInput").ap()
    biasb = nc.dram_tensor("biasb", [128, D], F32, kind="ExternalInput").ap()
    idf = nc.dram_tensor("idf", [128, 128], F32, kind="ExternalInput").ap()
    idb = nc.dram_tensor("idb", [128, 128], BF16, kind="ExternalInput").ap()
    masksrc = nc.dram_tensor("masks", [128, 1160], BF16, kind="ExternalInput").ap()
    y = nc.dram_tensor("y", [TOK, D], F32, kind="ExternalOutput").ap()
    xp = nc.dram_tensor("xp", [TOK, D], F32, kind="Internal").ap()
    wqbf = nc.dram_tensor("wqbf", [D, D], BF16, kind="Internal").ap()
    wkbf = nc.dram_tensor("wkbf", [D, 512], BF16, kind="Internal").ap()
    wvbf = nc.dram_tensor("wvbf", [D, 512], BF16, kind="Internal").ap()
    wobf = nc.dram_tensor("wobf", [D, D], BF16, kind="Internal").ap()
    wfbf = nc.dram_tensor("wfbf", [D, D], BF16, kind="Internal").ap()
    wscl = nc.dram_tensor("wscl", [128, D], F32, kind="Internal").ap()
    if debug:
        dxp = nc.dram_tensor("dxp", [TOK, D], F32, kind="ExternalOutput").ap()
        dkf = nc.dram_tensor("dkf", [128, NKV * KCOLS], BF16, kind="ExternalOutput").ap()
        dvt = nc.dram_tensor("dvt", [128, NKV * KCOLS], BF16, kind="ExternalOutput").ap()
        dq = nc.dram_tensor("dq", [128, 16384], BF16, kind="ExternalOutput").ap()
        dm = nc.dram_tensor("dm", [128, 16384], BF16, kind="ExternalOutput").ap()
        dwf = nc.dram_tensor("dwf", [D, D], BF16, kind="ExternalOutput").ap()
        dws = nc.dram_tensor("dws", [128, D], F32, kind="ExternalOutput").ap()
        dstat = nc.dram_tensor("dstat", [128, 16], F32, kind="ExternalOutput").ap()
        dxs0 = nc.dram_tensor("dxs0", [128, D], F32, kind="ExternalOutput").ap()
        dxs1 = nc.dram_tensor("dxs1", [128, D], F32, kind="ExternalOutput").ap()
        dxnt1 = nc.dram_tensor("dxnt1", [128, D], BF16, kind="ExternalOutput").ap()
        dwk = nc.dram_tensor("dwk", [D, 512], BF16, kind="ExternalOutput").ap()
        dwo = nc.dram_tensor("dwo", [D, D], BF16, kind="ExternalOutput").ap()

    with ExitStack() as stk:
        def S(*a, **k):
            return stk.enter_context(nc.sbuf_tensor(*a, **k)).ap()

        arena_w = S("arena_w", [128, 16 * 2048], BF16)
        arena_q = S("arena_q", [128, 16 * 1024], BF16)
        arena_m = S("arena_m", [128, 16 * 1024], BF16)
        am32 = arena_m.bitcast(F32)
        kf = S("kf", [128, NKV * KCOLS], BF16)
        vt = S("vt", [128, NKV * KCOLS], BF16)
        ksink = S("ksink", [128, 16], BF16)
        vsink = S("vsink", [128, NKV * 128], BF16)
        xstage = [S(f"xstage{i}", [128, D], F32) for i in range(2)]
        xnt = [S(f"xnt{i}", [128, D], BF16) for i in range(2)]
        wstage = [S(f"wstage{i}", [128, D], F32) for i in range(2)]
        xstage.append(wstage[0])
        sqs = S("sqs", [128, 512], F32)
        exb = [S(f"exb{i}", [128, 388], BF16) for i in range(4)]
        pmb = [S(f"pmb{i}", [128, 388], BF16) for i in range(4)]
        diag = [S(f"diag{i}", [128, 128], BF16) for i in range(4)]
        negm = S("negm", [128, 4], F32)
        den = S("den", [128, 4], F32)
        alphab = S("alphab", [128, 4], BF16)
        statc = S("statc", [128, 24], F32)
        epsc = S("epsc", [128, 1], F32)
        jnk = S("jnk", [128, 1], F32)
        onesr = S("onesr", [1, 128], F32)
        maskb = S("maskb", [128, 1160], BF16)
        idbuf = S("idbuf", [128, 128], BF16)
        idfb = S("idfb", [128, 128], F32)
        wmaxt = S("wmaxt", [128, 16], F32)
        wrec = S("wrec", [128, 16], F32)
        wsclT = S("wsclT", [128, 16], F32)
        w8st = S("w8st", [128, D], I8)
        xq8 = w8st
        tmpb = None

        def xn_f(dd):
            return arena_m[:, dd * 1024:(dd + 1) * 1024]
        am3 = arena_m.rearrange("p (c n) -> p c n", n=1024)
        ptbuf = [arena_m[:, 0:2048], arena_m[:, 2048:4096]]
        ring = [arena_m[:, 4096:6144], arena_m[:, 6144:8192]]
        xps = [am32[:, 4096:6144], am32[:, 6144:8192]]
        wsb = am32[:, 0:2048]
        bib = am32[:, 2048:4096]
        outst = am32[:, 4096:6144]
        xqtb = [arena_m[:, 12288:14336], arena_m[:, 14336:16384]]

        wsrow = am32[0:1, 6144:8192]
        tmpv = [xnt[sl // 2].bitcast(F32)[:, (sl % 2) * 512:(sl % 2 + 1) * 512] for sl in range(4)]
        pbank = [stk.enter_context(nc.psum_tensor(f"ps{i}", [128, 512], F32)).ap() for i in range(8)]

        dL = Sem(stk, nc, "dL")
        dO = Sem(stk, nc, "dO")
        dG = Sem(stk, nc, "dG")
        tE = Sem(stk, nc, "tE")
        vE = Sem(stk, nc, "vE")
        sE = Sem(stk, nc, "sE")
        gE = Sem(stk, nc, "gE")
        p = Prog()

        guard = {}

        def G(key):
            g = guard.get(key)
            return [g] if g else []

        def setg(key, sem, val):
            guard[key] = (sem, val)

        BK = [f"bank{i}" for i in range(8)]

        for dst, src in [(idfb, idf), (idbuf, idb), (maskb, masksrc)]:
            p.emit("sync", lambda h, d=dst, s=src: h.dma_start(out=d, in_=s), inc=dL, inck=16)
        c_consts = dL.n
        p.emit("vector", lambda h: h.memset(epsc, EPS))
        p.emit("vector", lambda h: h.memset(onesr, 1.0))
        for i in range(4):
            p.emit("vector", lambda h, i=i: h.memset(diag[i], 0.0))

        prepv = {}

        def emit_ffn_prep():
            kst = [arena_m[:, 4096:4608], arena_m[:, 4608:5120]]
            wfb3 = wfbf.rearrange("(c p) n -> p c n", p=128)
            last_sp = 0
            for t in range(16):
                w = p.emit("sync", lambda h, t=t: h.dma_start(out=wstage[t % 2], in_=wfO[t * 128:(t + 1) * 128, :]),
                           waits=G(f"wst{t % 2}"), inc=dL, inck=16)
                vm = p.emit("vector", lambda h, t=t: h.reduce_max(out=wmaxt[:, t:t + 1], in_=wstage[t % 2],
                                                                  axis=AX.X, apply_absolute_value=True),
                           waits=[(dL, w)], inc=vE)
                p.emit("vector", lambda h: h.tensor_copy(out=jnk, in_=epsc))
                p.emit("vector", lambda h, t=t: h.tensor_scalar(
                    out=wsclT[:, t:t + 1], in0=wmaxt[:, t:t + 1], scalar1=1e-8, scalar2=INV127,
                    op0=ALU.max, op1=ALU.mult), inc=vE)
                p.emit("vector", lambda h: h.tensor_copy(out=jnk, in_=epsc))
                def _wrec(h, t=t):
                    return h.reciprocal(out=wrec[:, t:t + 1], in_=wsclT[:, t:t + 1])
                vrec = p.emit("vector", _wrec, inc=vE)
                s2l = []
                s1 = s2 = None
                for hh2 in range(4):
                    s1 = p.emit("scalar", lambda h, t=t, hh2=hh2: h.activation(
                        out=w8st[:, hh2 * 512:(hh2 + 1) * 512],
                        in_=wstage[t % 2][:, hh2 * 512:(hh2 + 1) * 512], func=AF.Identity,
                        scale=wrec[:, t:t + 1]),
                                waits=([(vE, vrec)] + G("w8st")) if hh2 == 0 else [], inc=sE)
                    s2 = p.emit("scalar", lambda h, hh2=hh2: h.activation(
                        out=xnt[0][:, hh2 * 512:(hh2 + 1) * 512],
                        in_=w8st[:, hh2 * 512:(hh2 + 1) * 512], func=AF.Identity, scale=1.0),
                                waits=G("xnt0w") if hh2 == 0 else [], inc=sE)
                    s2l.append(s2)
                setg(f"wst{t % 2}", sE, s1)
                setg("w8st", sE, s2)
                for half in range(4):
                    bi = 6 + half % 2
                    bank = pbank[bi]
                    tw = None
                    for j in range(4):
                        dd = half * 4 + j
                        tw = p.emit("tensor", lambda h, dd=dd, j=j, bank=bank:
                                    h.matmul(out=bank[:, j * 128:(j + 1) * 128],
                                             lhsT=xnt[0][:, dd * 128:(dd + 1) * 128], rhs=idbuf,
                                             start=True, stop=True),
                                    waits=([(sE, s2l[half]), (dL, c_consts)] + G(BK[bi])) if j == 0 else [],
                                    inc=tE if j == 3 else None)
                    ev = p.emit("vector", lambda h, half=half, bank=bank:
                                h.tensor_copy(out=kst[half % 2].rearrange("p (c n) -> p c n", n=128),
                                              in_=bank),
                                waits=[(tE, tw)] + G(f"kst{half % 2}") + G("prepm"), inc=vE)
                    setg(BK[bi], vE, ev)
                    sp = p.emit("sync", lambda h, t=t, half=half: h.dma_start(
                        out=wfb3[:, half * 4:(half + 1) * 4, t * 128:(t + 1) * 128],
                        in_=kst[half % 2].rearrange("p (c n) -> p c n", n=128)),
                                waits=[(vE, ev)], inc=dO, inck=16)
                    setg(f"kst{half % 2}", dO, sp)
                    last_sp = sp
                setg("xnt0w", vE, vE.n)
                m1 = p.emit("tensor", lambda h, t=t: h.matmul(out=pbank[6][0:1, 0:128], lhsT=wsclT[:, t:t + 1],
                                                              rhs=idfb, start=True, stop=True),
                            waits=[(vE, vrec), (dL, c_consts)] + G(BK[6]), inc=tE)
                e1 = p.emit("vector", lambda h, t=t: h.tensor_copy(out=wsrow[0:1, t * 128:(t + 1) * 128],
                                                                   in_=pbank[6][0:1, 0:128]),
                            waits=[(tE, m1)], inc=vE)
                setg(BK[6], vE, e1)
                m2 = p.emit("tensor", lambda h, t=t: h.matmul(out=pbank[7][:, 0:128], lhsT=onesr,
                                                              rhs=wsrow[0:1, t * 128:(t + 1) * 128],
                                                              start=True, stop=True),
                            waits=[(vE, e1)] + G(BK[7]), inc=tE)
                e2 = p.emit("vector", lambda h, t=t: h.tensor_copy(out=am32[:, t * 128:(t + 1) * 128],
                                                                   in_=pbank[7][:, 0:128]),
                            waits=[(tE, m2)] + G("prepm"), inc=vE)
                setg(BK[7], vE, e2)
            prepv['wf_spill_done'] = last_sp
            wscl_spill = p.emit("sync", lambda h: h.dma_start(out=wscl, in_=am32[:, 0:2048]),
                                waits=[(vE, vE.n)], inc=dO, inck=16)
            prepv['wscl_spill'] = wscl_spill
            setg("wsrow_rd", dO, wscl_spill)

        spill_record = {}

        for s in range(nstripe):
            tok0 = s * STRIPE
            first = (s == 0)
            seq_start = (s % 2 == 0)

            xload = [None] * NQT

            def emit_xload(qt, src_t, extra=(), nb=2):
                idx = qt % nb
                xload[qt] = p.emit("gpsimd", lambda h, qt=qt, src_t=src_t, tok0=tok0, idx=idx: h.dma_start(
                    out=xstage[idx], in_=src_t[tok0 + qt * 128: tok0 + (qt + 1) * 128, :]),
                           waits=G(f"xst{idx}") + list(extra), inc=dG, inck=16)
                guard.pop(f"xst{idx}", None)

            emit_xload(0, x)
            emit_xload(1, x)

            def load_weight(dst_fn, n_tiles, width, fsrc, bfscr, tag, extra=()):
                done = []
                if first:
                    spills = []
                    for t in range(n_tiles):
                        w = p.emit("sync", lambda h, t=t, width=width, fsrc=fsrc: h.dma_start(
                            out=wstage[t % 2][:, 0:width], in_=fsrc[t * 128:(t + 1) * 128, :]),
                                   waits=G(f"wst{t % 2}"), inc=dL, inck=16)
                        if t % 2 == 0:
                            c = p.emit("vector", lambda h, t=t, width=width: h.tensor_copy(
                                out=dst_fn(t), in_=wstage[t % 2][:, 0:width]),
                                       waits=[(dL, w)] + G(f"{tag}{t}") + list(extra), inc=vE)
                            csem = vE
                        else:
                            c = p.emit("scalar", lambda h, t=t, width=width: h.activation(
                                out=dst_fn(t), in_=wstage[t % 2][:, 0:width], func=AF.Identity, scale=1.0),
                                       waits=[(dL, w)] + G(f"{tag}{t}") + list(extra), inc=sE)
                            csem = sE
                        setg(f"wst{t % 2}", csem, c)
                        sp = p.emit("sync", lambda h, t=t, bfscr=bfscr: h.dma_start(
                            out=bfscr[t * 128:(t + 1) * 128, :], in_=dst_fn(t)),
                                    waits=[(csem, c)], inc=dO, inck=16)
                        spills.append(sp)
                        done.append((csem, c))
                    spill_record[tag] = spills[-1]
                else:
                    for t in range(n_tiles):
                        w = p.emit("sync", lambda h, t=t, bfscr=bfscr: h.dma_start(
                            out=dst_fn(t), in_=bfscr[t * 128:(t + 1) * 128, :]),
                                   waits=G(f"{tag}{t}") + [(dO, spill_record[tag])] + list(extra),
                                   inc=dL, inck=16)
                        done.append((dL, w))
                return done

            wk_tile = lambda t: arena_q[:, t * 512:(t + 1) * 512]
            wv_tile = lambda t: arena_q[:, 8192 + t * 512: 8192 + (t + 1) * 512]
            w_tile = lambda t: arena_w[:, t * 2048:(t + 1) * 2048]

            ln1_done = [None] * NQT
            ln1_s3 = {}

            def ln1_front(qt):
                c0 = qt % 2
                base = c0 * 4
                w = xload[qt]
                v1 = p.emit("vector", lambda h, qt=qt, base=base: h.reduce_sum(
                    out=statc[:, base:base + 1], in_=xstage[qt % 2], axis=AX.X),
                            waits=[(dG, w)] + G(f"mcol{base}"), inc=vE)
                sq = None
                for j in range(4):
                    sq = p.emit("scalar", lambda h, j=j, qt=qt: h.activation(
                        out=sqs, in_=xstage[qt % 2][:, j * 512:(j + 1) * 512], func=AF.Square, scale=1.0,
                        accum_out=statc[:, 8 + j:9 + j]),
                        waits=[(dG, w)] + (G("sqacc") if j == 0 else []), inc=sE)
                v2 = p.emit("vector", lambda h, base=base: h.reduce_sum(
                    out=statc[:, base + 1:base + 2], in_=statc[:, 8:12], axis=AX.X),
                            waits=[(sE, sq)] + G(f"var{c0}"), inc=vE)
                setg("sqacc", vE, v2)
                v3 = p.emit("vector", lambda h, base=base: h.tensor_scalar_mul(
                    out=statc[:, base:base + 1], in0=statc[:, base:base + 1], scalar1=INVD), inc=vE)
                v4 = p.emit("vector", lambda h, base=base: h.tensor_scalar_mul(
                    out=statc[:, base + 1:base + 2], in0=statc[:, base + 1:base + 2], scalar1=INVD), inc=vE)
                v5 = p.emit("vector", lambda h, base=base: h.tensor_tensor(
                    out=statc[:, 12:13], in0=statc[:, base:base + 1], in1=statc[:, base:base + 1],
                    op=ALU.mult),
                            waits=G("col12") + G("col12r"), inc=vE)
                p.emit("vector", lambda h: h.tensor_copy(out=jnk, in_=epsc))
                v6 = p.emit("vector", lambda h, base=base: h.scalar_tensor_tensor(
                    out=statc[:, base + 1:base + 2], in0=statc[:, 12:13], scalar=-1.0,
                    in1=statc[:, base + 1:base + 2], op0=ALU.mult, op1=ALU.add), inc=vE)
                s1 = p.emit("scalar", lambda h, base=base: h.activation(
                    out=statc[:, 12:13], in_=statc[:, base + 1:base + 2], func=AF.Ln, bias=epsc, scale=1.0),
                            waits=[(vE, v6)], inc=sE)
                setg(f"var{c0}", sE, s1)
                setg("col12", sE, s1)
                s2 = p.emit("scalar", lambda h, base=base: h.activation(
                    out=statc[:, base + 2:base + 3], in_=statc[:, 12:13], func=AF.Exp, scale=-0.5),
                            waits=G(f"rs{c0}"), inc=sE)
                setg("col12r", sE, s2)
                v7 = p.emit("vector", lambda h, base=base: h.scalar_tensor_tensor(
                    out=statc[:, base + 3:base + 4], in0=statc[:, base:base + 1], scalar=-1.0,
                    in1=statc[:, base + 2:base + 3], op0=ALU.mult, op1=ALU.mult),
                            waits=[(sE, s2)] + G(f"nb{c0}"), inc=vE)
                s3 = p.emit("scalar", lambda h, qt=qt, base=base: h.activation(
                    out=xnt[qt % 2], in_=xstage[qt % 2], func=AF.Identity,
                    scale=statc[:, base + 2:base + 3], bias=statc[:, base + 3:base + 4]),
                            waits=[(vE, v7)] + G(f"xnt{qt % 2}"), inc=sE)
                setg(f"rs{c0}", sE, s3)
                setg(f"nb{c0}", sE, s3)
                setg(f"mcol{base}", sE, s3)
                setg(f"xst{qt % 2}", sE, s3)
                if qt + 2 < NQT:
                    emit_xload(qt + 2, x)
                ln1_s3[qt] = s3

            def ln1_back(qt):
                s3 = ln1_s3.pop(qt)
                lastev = None
                for half in range(4):
                    bi = 4 + half % 2
                    bank = pbank[bi]
                    tw = None
                    for j in range(4):
                        dd = half * 4 + j
                        tw = p.emit("tensor", lambda h, dd=dd, j=j, bank=bank, qt=qt:
                                    h.matmul(out=bank[:, j * 128:(j + 1) * 128],
                                             lhsT=xnt[qt % 2][:, dd * 128:(dd + 1) * 128], rhs=idbuf,
                                             start=True, stop=True),
                                    waits=([(sE, s3), (dL, c_consts)] + G(BK[bi])) if j == 0 else [],
                                    inc=tE if j == 3 else None)
                    ev = p.emit("vector", lambda h, half=half, bank=bank, qt=qt:
                                h.tensor_copy(out=am3[:, half * 4:(half + 1) * 4, qt * 128:(qt + 1) * 128],
                                              in_=bank),
                                waits=[(tE, tw)] + (G("misc_ln") if (qt == 0 and half == 0) else []), inc=vE)
                    setg(BK[bi], vE, ev)
                    lastev = ev
                setg(f"xnt{qt % 2}", vE, lastev)
                ln1_done[qt] = lastev

            ln1_front(0)
            for qt in range(NQT):
                if qt + 1 < NQT:
                    ln1_front(qt + 1)
                ln1_back(qt)
            xnf_all = max(ln1_done)

            wk_done = load_weight(wk_tile, 16, 512, wkT, wkbf, "tk")
            wv_done = load_weight(wv_tile, 16, 512, wvT, wvbf, "tv")
            wq_done = load_weight(w_tile, 16, 2048, wqT, wqbf, "tq")

            if maxphase < 1:
                continue
            tail_done = 0
            if not seq_start:
                for c in range(NKV):
                    p.emit("vector", lambda h, c=c: h.tensor_copy(
                        out=kf[:, c * KCOLS: c * KCOLS + 256], in_=kf[:, c * KCOLS + 1024: c * KCOLS + 1280]),
                           waits=G("kf_rw"), inc=vE)
                    tail_done = p.emit("vector", lambda h, c=c: h.tensor_copy(
                        out=vt[:, c * KCOLS: c * KCOLS + 256], in_=vt[:, c * KCOLS + 1024: c * KCOLS + 1280]),
                           waits=G("vt_rw"), inc=vE)

            kwaits = [wk_done[dd] for dd in range(DK)] + [(vE, xnf_all)]
            kv_ready = 0
            for c in range(NKV):
                for st2 in range(2):
                    bi = (c * 2 + st2) % 2
                    bank = pbank[bi]
                    mm = None
                    for dd in range(DK):
                        mm = p.emit("tensor", lambda h, dd=dd, c=c, st2=st2, bank=bank:
                                    h.matmul(out=bank, lhsT=wk_tile(dd)[:, c * 128:(c + 1) * 128],
                                             rhs=xn_f(dd)[:, st2 * 512:(st2 + 1) * 512],
                                             start=(dd == 0), stop=(dd == DK - 1)),
                                    waits=(kwaits + G(BK[bi])) if dd == 0 else [],
                                    inc=tE if dd == DK - 1 else None)
                    ev = p.emit("vector", lambda h, c=c, st2=st2, bank=bank: h.tensor_copy(
                        out=kf[:, c * KCOLS + 256 + st2 * 512: c * KCOLS + 256 + (st2 + 1) * 512], in_=bank),
                                waits=[(tE, mm), (vE, tail_done)] + G("kf_rw"), inc=vE)
                    setg(BK[bi], vE, ev)
                    kv_ready = ev
                if seq_start:
                    p.emit("vector", lambda h: h.tensor_copy(out=jnk, in_=epsc))
                    kv_ready = p.emit("vector", lambda h, c=c: h.tensor_copy(
                        out=ksink[:, c * 4:(c + 1) * 4], in_=kf[:, c * KCOLS + 256: c * KCOLS + 260]),
                        waits=G("kf_rw"), inc=vE)

            vt3 = vt.rearrange("p (c n) -> p c n", n=KCOLS)
            vwaits = [wv_done[dd] for dd in range(DK)] + [(vE, xnf_all)]
            for qt in range(NQT):
                bi = 2 + qt % 2
                bank = pbank[bi]
                mm = None
                for dd in range(DK):
                    mm = p.emit("tensor", lambda h, dd=dd, qt=qt, bank=bank:
                                h.matmul(out=bank, lhsT=xn_f(dd)[:, qt * 128:(qt + 1) * 128],
                                         rhs=wv_tile(dd),
                                         start=(dd == 0), stop=(dd == DK - 1)),
                                waits=(vwaits + G(BK[bi])) if dd == 0 else [],
                                inc=tE if dd == DK - 1 else None)
                ev = p.emit("vector", lambda h, qt=qt, bank=bank: h.tensor_copy(
                    out=vt3[:, :, (2 + qt) * 128:(3 + qt) * 128], in_=bank),
                            waits=[(tE, mm), (vE, tail_done)] + G("vt_rw"), inc=vE)
                setg(BK[bi], vE, ev)
                if seq_start and qt == 0:
                    p.emit("vector", lambda h: h.tensor_copy(out=jnk, in_=epsc))
                    p.emit("vector", lambda h: h.tensor_copy(
                        out=vsink.rearrange("p (c n) -> p c n", n=128)[0:4, :, :],
                        in_=vt3[0:4, :, 256:384]),
                           waits=[(vE, ev)] + G("vt_rw"), inc=vE)
            v_all = vE.n
            kv_ready = max(kv_ready, v_all)

            if maxphase < 2:
                continue
            qwaits = [wq_done[dd] for dd in range(DK)] + [(vE, xnf_all)]
            kv_mm_end = tE.n
            qf_ready = 0
            for hh in range(NH):
                for st2 in range(2):
                    bi = (hh * 2 + st2) % 2
                    bank = pbank[bi]
                    mm = None
                    for dd in range(DK):
                        mm = p.emit("tensor", lambda h, dd=dd, hh=hh, st2=st2, bank=bank:
                                    h.matmul(out=bank, lhsT=w_tile(dd)[:, hh * 128:(hh + 1) * 128],
                                             rhs=xn_f(dd)[:, st2 * 512:(st2 + 1) * 512],
                                             start=(dd == 0), stop=(dd == DK - 1)),
                                    waits=(qwaits + G(BK[bi])) if dd == 0 else [],
                                    inc=tE if dd == DK - 1 else None)
                    qf_ready = p.emit("vector", lambda h, hh=hh, st2=st2, bank=bank: h.tensor_scalar_mul(
                        out=arena_q[:, hh * 1024 + st2 * 512: hh * 1024 + (st2 + 1) * 512],
                        in0=bank, scalar1=SCALE),
                                waits=[(tE, mm), (tE, kv_mm_end),
                                       (dO, spill_record.get("tk", 0)), (dO, spill_record.get("tv", 0))]
                                      + G(BK[bi] + "x"), inc=vE)
                    setg(BK[bi], vE, qf_ready)
            q_mm_end = tE.n
            if first:
                setg("xnt0w", vE, xnf_all)
                setg("prepm", tE, q_mm_end)
                emit_ffn_prep()
                setg("vt_rw", tE, tE.n)

            if maxphase < 3:
                continue
            for t in range(16):
                setg(f"tq{t}", tE, q_mm_end)
            wo_done = load_weight(w_tile, 16, 2048, woT, wobf, "tq",
                                  extra=[(dO, spill_record.get("tq", 0))])

            xrel = xload
            emit_xload(0, x)
            emit_xload(1, x)

            misc_attn = (tE, q_mm_end)
            if first:
                misc_attn = (dO, prepv['wscl_spill'])
            pend_o = None
            o_state = {}

            def emit_o_slice(oqt, sl, ring_ready):
                bi4 = 6 + sl % 2
                bank5 = pbank[bi4]
                mm = None
                for hh in range(NH):
                    mm = p.emit("tensor", lambda h, hh=hh, sl=sl, bank5=bank5, oqt=oqt:
                                h.matmul(out=bank5, lhsT=ring[oqt % 2][:, hh * 128:(hh + 1) * 128],
                                         rhs=w_tile(hh)[:, sl * 512:(sl + 1) * 512],
                                         start=(hh == 0), stop=(hh == NH - 1)),
                                waits=([(vE, ring_ready)] + [wo_done[t] for t in range(16)]
                                       + G(BK[bi4])) if hh == 0 else [],
                                inc=tE if hh == NH - 1 else None)
                xv = p.emit("vector", lambda h, sl=sl, oqt=oqt, bank5=bank5: h.tensor_tensor(
                    out=xps[oqt % 2][:, sl * 512:(sl + 1) * 512], in0=bank5,
                    in1=xstage[oqt % 2][:, sl * 512:(sl + 1) * 512], op=ALU.add),
                            waits=[(tE, mm), (dG, xrel[oqt])]
                                  + (G(f"xps{oqt % 2}") if sl == 0 else [])
                                  + ([misc_attn] if oqt <= 1 else []),
                            inc=vE)
                setg(BK[bi4], vE, xv)
                o_state[oqt] = xv

            def emit_o_finish(oqt, tok0=tok0):
                xv_last = o_state.pop(oqt)
                setg(f"ring{oqt % 2}", tE, tE.n)
                setg(f"xst{oqt % 2}", vE, xv_last)
                if oqt + 2 < NQT:
                    emit_xload(oqt + 2, x)
                sp = p.emit("sync", lambda h, oqt=oqt, tok0=tok0: h.dma_start(
                    out=xp[tok0 + oqt * 128: tok0 + (oqt + 1) * 128, :], in_=xps[oqt % 2]),
                            waits=[(vE, xv_last)], inc=dO, inck=16)
                setg(f"xps{oqt % 2}", dO, sp)

            for qt in range(NQT):
                g = (s % 2) * NQT + qt
                if g == 0:
                    W, nch, has_sink, moff = 128, 1, False, 0
                elif g == 1:
                    W, nch, has_sink, moff = 256, 2, False, 128
                else:
                    W, nch, has_sink, moff = 384, 3, True, (384 if g == 2 else 772)
                kstart = 256 if g <= 1 else qt * 128
                wtot = W + (4 if has_sink else 0)
                nall = nch + (1 if has_sink else 0)

                st = {}
                pend_pts = []
                group_ev = {}

                def emit_pt(u, qt=qt, nch=nch, has_sink=has_sink, nall=nall):
                    kv, j = u // 4, u % 4
                    dg = st[u]
                    bi2 = 2 + u % 2
                    bank2 = pbank[bi2]
                    for cidx in range(nch):
                        p.emit("tensor", lambda h, cidx=cidx, u=u, bank2=bank2:
                               h.matmul(out=bank2[:, cidx * 128:(cidx + 1) * 128],
                                        lhsT=exb[u % 4][:, cidx * 128:(cidx + 1) * 128],
                                        rhs=diag[u % 4], start=True, stop=True),
                               waits=([(gE, dg)] + G(BK[bi2])) if cidx == 0 else [],
                               inc=tE if (cidx == nch - 1 and not has_sink) else None)
                    if has_sink:
                        p.emit("tensor", lambda h, u=u, bank2=bank2:
                               h.matmul(out=bank2[0:4, 384:512],
                                        lhsT=exb[u % 4][:, 384:388], rhs=diag[u % 4],
                                        start=True, stop=True),
                               inc=tE)
                    ptv = tE.n
                    setg(f"exb{u % 4}", tE, ptv)
                    setg(f"diag{u % 4}", tE, ptv)
                    pt3 = ptbuf[kv % 2].rearrange("p (c n) -> p c n", n=512)
                    bank3 = bank2.rearrange("p (c n) -> p c n", n=128)
                    pe = p.emit("scalar", lambda h, j=j, pt3=pt3, bank3=bank3, nall=nall:
                                h.activation(out=pt3[:, 0:nall, j * 128:(j + 1) * 128],
                                             in_=bank3[:, 0:nall, :], func=AF.Identity, scale=1.0),
                                waits=[(tE, ptv)]
                                      + (G(f"ptbuf{kv % 2}") if j == 0 else [])
                                      + ([misc_attn] if (qt == 0 and kv < 2) else []),
                                inc=sE)
                    setg(BK[bi2], sE, pe)
                    st[u] = pe

                def emit_av(kv, qt=qt, g=g, nch=nch, has_sink=has_sink):
                    bi3 = 4
                    bank4 = pbank[bi3]
                    evs = max(st[kv * 4 + j] for j in range(4))
                    for cidx in range(nch):
                        blk = (2 + cidx) if g <= 1 else (qt + cidx)
                        p.emit("tensor", lambda h, cidx=cidx, kv=kv, blk=blk, bank4=bank4:
                               h.matmul(out=bank4,
                                        lhsT=vt[:, kv * KCOLS + blk * 128: kv * KCOLS + (blk + 1) * 128],
                                        rhs=ptbuf[kv % 2][:, cidx * 512:(cidx + 1) * 512],
                                        start=(cidx == 0), stop=(cidx == nch - 1 and not has_sink)),
                               waits=([(sE, evs), (vE, v_all)] + G(BK[bi3])) if cidx == 0 else [],
                               inc=tE if (cidx == nch - 1 and not has_sink) else None)
                    if has_sink:
                        p.emit("tensor", lambda h, kv=kv, bank4=bank4:
                               h.matmul(out=bank4, lhsT=vsink[0:4, kv * 128:(kv + 1) * 128],
                                        rhs=ptbuf[kv % 2][0:4, 1536:2048],
                                        start=False, stop=True),
                               inc=tE)
                    av_t = tE.n
                    setg(f"ptbuf{kv % 2}", tE, av_t)
                    rv = p.emit("vector", lambda h, kv=kv, qt=qt, bank4=bank4: h.tensor_copy(
                        out=ring[qt % 2][:, kv * 512:(kv + 1) * 512], in_=bank4),
                                waits=[(tE, av_t)] + (G(f"ring{qt % 2}") if kv == 0 else [])
                                      + ([misc_attn] if qt == 0 else []),
                                inc=vE)
                    setg(BK[bi3], vE, rv)
                    group_ev[kv] = rv

                for kv in range(NKV):
                    for j in range(4):
                        u = kv * 4 + j
                        hh = kv * 4 + j
                        bi = [0, 1, 5][u % 3]
                        bank = pbank[bi]
                        mm = p.emit("tensor", lambda h, hh=hh, kv=kv, bank=bank, W=W, kstart=kstart, qt=qt:
                                    h.matmul(out=bank[:, 0:W],
                                             lhsT=arena_q[:, hh * 1024 + qt * 128: hh * 1024 + qt * 128 + 128],
                                             rhs=kf[:, kv * KCOLS + kstart: kv * KCOLS + kstart + W],
                                             start=True, stop=False),
                                    waits=[(vE, qf_ready), (vE, kv_ready)] + G(BK[bi]))
                        if has_sink:
                            mm = p.emit("tensor", lambda h, hh=hh, kv=kv, bank=bank, qt=qt:
                                        h.matmul(out=bank[:, 384:388],
                                                 lhsT=arena_q[:, hh * 1024 + qt * 128: hh * 1024 + qt * 128 + 128],
                                                 rhs=ksink[:, kv * 4:(kv + 1) * 4],
                                                 start=False, stop=False))
                        mm = p.emit("tensor", lambda h, bank=bank, wtot=wtot, moff=moff:
                                    h.matmul(out=bank[:, 0:wtot], lhsT=idbuf,
                                             rhs=maskb[:, moff:moff + wtot],
                                             start=False, stop=True),
                                    waits=[(dL, c_consts)], inc=tE)
                        sc = tE.n
                        mx = p.emit("vector", lambda h, u=u, bank=bank, wtot=wtot: h.reduce_max(
                            out=negm[:, u % 4:u % 4 + 1], in_=bank[:, 0:wtot], axis=AX.X, negate=True),
                                    waits=[(tE, sc)] + G(f"negm{u % 4}"), inc=vE)
                        ex_ = p.emit("scalar", lambda h, u=u, bank=bank, wtot=wtot: h.activation(
                            out=exb[u % 4][:, 0:wtot], in_=bank[:, 0:wtot], func=AF.Exp,
                            bias=negm[:, u % 4:u % 4 + 1], scale=1.0,
                            accum_out=den[:, u % 4:u % 4 + 1]),
                                     waits=[(vE, mx)] + G(f"exb{u % 4}") + G(f"den{u % 4}"), inc=sE)
                        setg(BK[bi], sE, ex_)
                        setg(f"negm{u % 4}", sE, ex_)
                        def _recip(h, u=u):
                            with nc.allow_low_precision(reason="alpha in bf16 feeds bf16 matmul"):
                                return h.reciprocal(out=alphab[:, u % 4:u % 4 + 1],
                                                    in_=den[:, u % 4:u % 4 + 1])
                        rc = p.emit("vector", _recip, waits=[(sE, ex_)] + G(f"alf{u % 4}"), inc=vE)
                        setg(f"den{u % 4}", vE, rc)
                        dg = p.emit("gpsimd", lambda h, u=u: h.tensor_tensor(
                            out=diag[u % 4], in0=alphab[:, u % 4:u % 4 + 1].broadcast_to((128, 128)),
                            in1=idbuf, op=ALU.mult),
                                    waits=[(vE, rc), (dL, c_consts)] + G(f"diag{u % 4}"), inc=gE)
                        setg(f"alf{u % 4}", gE, dg)
                        st[u] = dg
                        pend_pts.append(u)
                        if len(pend_pts) > 3:
                            pu = pend_pts.pop(0)
                            emit_pt(pu)
                            if pu % 4 == 3:
                                emit_av(pu // 4)
                                if pend_o is not None:
                                    oqt, orr = pend_o
                                    emit_o_slice(oqt, pu // 4, orr)
                                    if pu // 4 == 3:
                                        emit_o_finish(oqt)
                                        pend_o = None
                for pu in pend_pts:
                    emit_pt(pu)
                    if pu % 4 == 3:
                        emit_av(pu // 4)
                        if pend_o is not None:
                            oqt, orr = pend_o
                            emit_o_slice(oqt, pu // 4, orr)
                            if pu // 4 == 3:
                                emit_o_finish(oqt)
                                pend_o = None
                pend_pts = []
                pend_o = (qt, max(group_ev.values()))
            oqt, orr = pend_o
            for sl in range(4):
                emit_o_slice(oqt, sl, orr)
            emit_o_finish(oqt)
            pend_o = None
            setg(f"xpsl{0}", dO, guard[f"xps{0}"][1])
            setg(f"xpsl{1}", dO, guard[f"xps{1}"][1])
            attn_te_end = tE.n

            if maxphase < 4:
                continue
            for t in range(16):
                setg(f"tq{t}", tE, attn_te_end)
            wf_done = []
            for t in range(16):
                w = p.emit("sync", lambda h, t=t: h.dma_start(out=w_tile(t), in_=wfbf[t * 128:(t + 1) * 128, :]),
                           waits=G(f"tq{t}") + [(dO, prepv['wf_spill_done']), (dO, spill_record.get("tq", 0))],
                           inc=dL, inck=16)
                wf_done.append((dL, w))
            setg("outst", dO, max(guard.get("xps0", (None, 0))[1], guard.get("xps1", (None, 0))[1]))
            wslw = p.emit("sync", lambda h: h.dma_start(out=wsb, in_=wscl),
                          waits=[(tE, attn_te_end), (dO, prepv['wscl_spill'])], inc=dL, inck=16)
            bblw = p.emit("sync", lambda h: h.dma_start(out=bib, in_=biasb),
                          waits=[(tE, attn_te_end)], inc=dL, inck=16)

            out_dma_last = 0
            emit_xload(0, xp, extra=G(f"xpsl0"), nb=3)
            emit_xload(1, xp, extra=G(f"xpsl1"), nb=3)
            emit_xload(2, xp, extra=G(f"xpsl0") + list(wo_done[-2:]), nb=3)
            ffn_lastev = {}
            ffn_sq2 = {}

            def ffn_stats(qt):
                pb = (qt % 3) * 4
                xw = xload[qt]
                v1 = p.emit("vector", lambda h, qt=qt, pb=pb: h.reduce_sum(
                    out=statc[:, pb + 3:pb + 4], in_=xstage[qt % 3], axis=AX.X),
                            waits=[(dG, xw)] + G(f"mf{qt % 3}"), inc=vE)
                sq = None
                for j in range(4):
                    sq = p.emit("scalar", lambda h, j=j, qt=qt: h.activation(
                        out=sqs, in_=xstage[qt % 3][:, j * 512:(j + 1) * 512], func=AF.Square, scale=1.0,
                        accum_out=statc[:, 16 + j:17 + j]),
                        waits=[(dG, xw)] + (G("sqaccF") if j == 0 else []), inc=sE)
                vmx = p.emit("vector", lambda h, qt=qt: h.reduce_max(
                    out=statc[:, 13:14], in_=xstage[qt % 3], axis=AX.X),
                             waits=[(dG, xw)] + G("col13"), inc=vE)
                vmn = p.emit("vector", lambda h, qt=qt: h.tensor_reduce(
                    out=statc[:, 14:15], in_=xstage[qt % 3], axis=AX.X, op=ALU.min),
                             waits=[(dG, xw)], inc=vE)
                v2 = p.emit("vector", lambda h: h.reduce_sum(out=statc[:, 15:16], in_=statc[:, 16:20], axis=AX.X),
                            waits=[(sE, sq)], inc=vE)
                setg("sqaccF", vE, v2)
                v3 = p.emit("vector", lambda h, pb=pb: h.tensor_scalar_mul(
                    out=statc[:, pb + 3:pb + 4], in0=statc[:, pb + 3:pb + 4], scalar1=INVD), inc=vE)
                v4 = p.emit("vector", lambda h: h.tensor_scalar_mul(
                    out=statc[:, 15:16], in0=statc[:, 15:16], scalar1=INVD), inc=vE)
                v5 = p.emit("vector", lambda h, pb=pb: h.tensor_tensor(
                    out=statc[:, 12:13], in0=statc[:, pb + 3:pb + 4], in1=statc[:, pb + 3:pb + 4],
                    op=ALU.mult),
                            waits=G("col12"), inc=vE)
                v7 = p.emit("vector", lambda h, pb=pb: h.tensor_tensor(
                    out=statc[:, 13:14], in0=statc[:, 13:14], in1=statc[:, pb + 3:pb + 4],
                    op=ALU.subtract),
                            waits=[(vE, vmx)], inc=vE)
                v6 = p.emit("vector", lambda h: h.scalar_tensor_tensor(
                    out=statc[:, 12:13], in0=statc[:, 12:13], scalar=-1.0, in1=statc[:, 15:16],
                    op0=ALU.mult, op1=ALU.add), inc=vE)
                v8 = p.emit("vector", lambda h, pb=pb: h.tensor_tensor(
                    out=statc[:, 14:15], in0=statc[:, pb + 3:pb + 4], in1=statc[:, 14:15],
                    op=ALU.subtract), inc=vE)
                s1 = p.emit("scalar", lambda h: h.activation(
                    out=statc[:, 12:13], in_=statc[:, 12:13], func=AF.Ln, bias=epsc, scale=1.0),
                            waits=[(vE, v6)], inc=sE)
                setg("col12", sE, s1)
                s2 = p.emit("scalar", lambda h: h.activation(
                    out=statc[:, 20:21], in_=statc[:, 12:13], func=AF.Exp, scale=-0.5),
                            waits=G("rs20"), inc=sE)
                p.emit("vector", lambda h: h.tensor_copy(out=jnk, in_=epsc))
                v9 = p.emit("vector", lambda h: h.tensor_tensor(
                    out=statc[:, 13:14], in0=statc[:, 13:14], in1=statc[:, 14:15], op=ALU.max), inc=vE)
                p.emit("vector", lambda h: h.tensor_copy(out=jnk, in_=epsc))
                v10 = p.emit("vector", lambda h: h.tensor_tensor(
                    out=statc[:, 13:14], in0=statc[:, 13:14], in1=statc[:, 20:21], op=ALU.mult),
                             waits=[(sE, s2)], inc=vE)
                p.emit("vector", lambda h: h.tensor_copy(out=jnk, in_=epsc))
                v11 = p.emit("vector", lambda h, pb=pb: h.tensor_scalar(
                    out=statc[:, pb:pb + 1], in0=statc[:, 13:14], scalar1=INV127, scalar2=1e-12,
                    op0=ALU.mult, op1=ALU.max), inc=vE)
                v12 = p.emit("vector", lambda h: h.tensor_copy(out=jnk, in_=epsc), inc=vE)
                v13 = p.emit("vector", lambda h, pb=pb: h.reciprocal(
                    out=statc[:, pb + 1:pb + 2], in_=statc[:, pb:pb + 1]),
                             waits=G(f"sbq{qt % 3}"), inc=vE)
                p.emit("vector", lambda h: h.tensor_copy(out=jnk, in_=epsc))
                v14 = p.emit("vector", lambda h, pb=pb: h.tensor_tensor(
                    out=statc[:, pb + 1:pb + 2], in0=statc[:, pb + 1:pb + 2],
                    in1=statc[:, 20:21], op=ALU.mult), inc=vE)
                setg("col13", vE, v14)
                setg("rs20", vE, v14)
                p.emit("vector", lambda h: h.tensor_copy(out=jnk, in_=epsc))
                v15 = p.emit("vector", lambda h, pb=pb: h.scalar_tensor_tensor(
                    out=statc[:, pb + 2:pb + 3], in0=statc[:, pb + 3:pb + 4], scalar=-1.0,
                    in1=statc[:, pb + 1:pb + 2], op0=ALU.mult, op1=ALU.mult), inc=vE)
                sq2l = []
                sq1 = sq2 = None
                for hh2 in range(4):
                    sq1 = p.emit("scalar", lambda h, qt=qt, pb=pb, hh2=hh2: h.activation(
                        out=xq8[:, hh2 * 512:(hh2 + 1) * 512],
                        in_=xstage[qt % 3][:, hh2 * 512:(hh2 + 1) * 512], func=AF.Identity,
                        scale=statc[:, pb + 1:pb + 2], bias=statc[:, pb + 2:pb + 3]),
                                 waits=([(vE, v15)] + G("xq8")) if hh2 == 0 else [], inc=sE)
                    sq2 = p.emit("scalar", lambda h, qt=qt, hh2=hh2: h.activation(
                        out=xqtb[qt % 2][:, hh2 * 512:(hh2 + 1) * 512],
                        in_=xq8[:, hh2 * 512:(hh2 + 1) * 512], func=AF.Identity, scale=1.0),
                                 waits=(G(f"xqt{qt % 2}") + G(f"xpsl{1}")
                                        + ([] if qt else (G("misc_ffn") + G("wsrow_rd")))) if hh2 == 0 else [],
                                 inc=sE)
                    sq2l.append(sq2)
                setg(f"sbq{qt % 3}", sE, sq1)
                setg(f"mf{qt % 3}", sE, sq1)
                setg("xq8", sE, sq2)
                ffn_sq2[qt] = sq2l

            def ffn_tr(qt):
                sq2l = ffn_sq2.pop(qt)
                lastev = None
                for half in range(4):
                    bi = 2 + half % 2
                    bank = pbank[bi]
                    tw = None
                    for j in range(4):
                        dd = half * 4 + j
                        tw = p.emit("tensor", lambda h, dd=dd, j=j, bank=bank, qt=qt:
                                    h.matmul(out=bank[:, j * 128:(j + 1) * 128],
                                             lhsT=xqtb[qt % 2][:, dd * 128:(dd + 1) * 128], rhs=idbuf,
                                             start=True, stop=True),
                                    waits=([(sE, sq2l[half])] + G(BK[bi])) if j == 0 else [],
                                    inc=tE if j == 3 else None)
                    aq3 = arena_q.rearrange("p (c n) -> p c n", n=1024)
                    ev = p.emit("vector", lambda h, half=half, bank=bank, qt=qt, aq3=aq3:
                                h.tensor_copy(out=aq3[:, half * 4:(half + 1) * 4, qt * 128:(qt + 1) * 128],
                                              in_=bank),
                                waits=[(tE, tw), (tE, attn_te_end),
                                       (dO, spill_record.get("tv", 0))], inc=vE)
                    setg(BK[bi], vE, ev)
                    lastev = ev
                setg(f"xqt{qt % 2}", vE, lastev)
                ffn_lastev[qt] = lastev

            def ffn_back(qt, mid=None):
                base = (qt % 3) * 4
                lastev = ffn_lastev.pop(qt)
                mmv = [None] * 4
                e1v = [None] * 4

                def ffn_mm(sl, qt=qt, lastev=lastev):
                    bi4 = 6 + sl % 2
                    bank5 = pbank[bi4]
                    mm = None
                    for dd in range(DK):
                        mm = p.emit("tensor", lambda h, dd=dd, sl=sl, bank5=bank5, qt=qt:
                                    h.matmul(out=bank5,
                                             lhsT=arena_q[:, dd * 1024 + qt * 128: dd * 1024 + qt * 128 + 128],
                                             rhs=w_tile(dd)[:, sl * 512:(sl + 1) * 512],
                                             start=(dd == 0), stop=(dd == DK - 1)),
                                    waits=([(vE, lastev), (dL, wslw), (dL, bblw)]
                                           + [wf_done[t] for t in range(16)]
                                           + G(BK[bi4])) if dd == 0 else [],
                                    inc=tE if dd == DK - 1 else None)
                    mmv[sl] = mm

                def ffn_e1(sl, base=base):
                    bi4 = 6 + sl % 2
                    e1v[sl] = p.emit("vector", lambda h, sl=sl, base=base: h.scalar_tensor_tensor(
                        out=outst[:, sl * 512:(sl + 1) * 512], in0=pbank[6 + sl % 2],
                        scalar=statc[:, base:base + 1], in1=wsb[:, sl * 512:(sl + 1) * 512],
                        op0=ALU.mult, op1=ALU.mult),
                                waits=[(tE, mmv[sl]), (dL, wslw)]
                                      + (G("outst") if sl == 0 else []), inc=vE)
                    setg(BK[bi4], vE, e1v[sl])

                ffn_mm(0); ffn_mm(1)
                ffn_e1(0)
                if mid is not None:
                    mid()
                ffn_mm(2)
                ffn_e1(1)
                ffn_mm(3)
                ffn_e1(2); ffn_e1(3)
                for sl in range(4):
                    p.emit("vector", lambda h, sl=sl: h.tensor_tensor(
                        out=tmpv[sl], in0=outst[:, sl * 512:(sl + 1) * 512],
                        in1=bib[:, sl * 512:(sl + 1) * 512], op=ALU.add),
                           waits=[(dL, bblw)] + (G("tmpv") if sl == 0 else []), inc=vE)
                out_writes = 0
                for sl in range(4):
                    out_writes = p.emit("vector", lambda h, sl=sl, qt=qt: h.tensor_tensor(
                        out=outst[:, sl * 512:(sl + 1) * 512], in0=tmpv[sl],
                        in1=xstage[qt % 3][:, sl * 512:(sl + 1) * 512], op=ALU.add), inc=vE)
                setg("tmpv", vE, out_writes)
                setg(f"xst{qt % 3}", vE, out_writes)
                if qt + 3 < NQT:
                    emit_xload(qt + 3, xp, extra=G(f"xpsl{(qt + 3) % 2}"), nb=3)
                ow = p.emit("sync", lambda h, qt=qt, tok0=tok0: h.dma_start(
                    out=y[tok0 + qt * 128: tok0 + (qt + 1) * 128, :], in_=outst),
                            waits=[(vE, out_writes)], inc=dO, inck=16)
                setg("outst", dO, ow)
                return ow

            ffn_stats(0)
            ffn_stats(1)
            ffn_tr(0)
            for qt in range(NQT):
                if qt + 2 < NQT:
                    ffn_stats(qt + 2)
                if qt + 1 < NQT:
                    out_dma_last = ffn_back(qt, mid=lambda qt=qt: ffn_tr(qt + 1))
                else:
                    out_dma_last = ffn_back(qt)
            ffn_te_end = tE.n
            setg("kf_rw", tE, attn_te_end)
            setg("vt_rw", tE, attn_te_end)
            setg("misc_ln", dO, out_dma_last)
            setg("xps0", dO, out_dma_last)
            setg("xps1", dO, out_dma_last)
            setg("outst", dO, out_dma_last)
            setg("misc_ffn", dO, guard.get("xps0", (dO, 0))[1])
            for t in range(16):
                setg(f"tq{t}", tE, ffn_te_end)
                setg(f"tk{t}", tE, ffn_te_end)
                setg(f"tv{t}", tE, ffn_te_end)
            setg("awq", tE, ffn_te_end)
            setg(BK[0] + "x", tE, ffn_te_end)
            setg(BK[1] + "x", tE, ffn_te_end)

        if debug:
            for dst, src_ in [(dkf, kf), (dvt, vt), (dq, arena_q), (dm, arena_m)]:
                p.emit("sync", lambda h, d=dst, s_=src_: h.dma_start(out=d, in_=s_),
                       waits=[(dO, dO.n), (tE, tE.n), (vE, vE.n), (sE, sE.n)], inc=dO, inck=16)
            p.emit("sync", lambda h: h.dma_start(out=dxp, in_=xp), inc=dO, inck=16)
            p.emit("sync", lambda h: h.dma_start(out=dstat, in_=statc[:, 0:16]), inc=dO, inck=16)
            p.emit("sync", lambda h: h.dma_start(out=dxs0, in_=xstage[0]), inc=dO, inck=16)
            p.emit("sync", lambda h: h.dma_start(out=dxs1, in_=xstage[1]), inc=dO, inck=16)
            p.emit("sync", lambda h: h.dma_start(out=dxnt1, in_=xnt[1]), inc=dO, inck=16)
            p.emit("sync", lambda h: h.dma_start(out=dwk, in_=wkbf), inc=dO, inck=16)
            p.emit("sync", lambda h: h.dma_start(out=dwo, in_=wobf), inc=dO, inck=16)
            p.emit("sync", lambda h: h.dma_start(out=dwf, in_=wfbf), inc=dO, inck=16)
            p.emit("sync", lambda h: h.dma_start(out=dws, in_=wscl), inc=dO, inck=16)
        final_dO = dO.n

        with nc.Block() as block:
            @block.sync
            def _(sync):
                for f in p.q["sync"]:
                    f(sync)
                sync.wait_ge(dO.h, final_dO)

            @block.tensor
            def _(tensor):
                for f in p.q["tensor"]:
                    f(tensor)

            @block.vector
            def _(vector):
                for f in p.q["vector"]:
                    f(vector)

            @block.scalar
            def _(scalar):
                for f in p.q["scalar"]:
                    f(scalar)

            @block.gpsimd
            def _(gpsimd):
                for f in p.q["gpsimd"]:
                    f(gpsimd)

    return nc


def _build_consts():
    r = np.arange(128)[:, None]
    j1 = np.arange(128)[None, :]
    j2 = np.arange(256)[None, :]
    j3 = np.arange(384)[None, :]
    m0 = (j1 <= r)
    m1 = (j2 <= 128 + r)
    band = (j3 >= r + 1) & (j3 < r + 257)
    s2 = (np.arange(4)[None, :] < np.minimum(4, r + 1))
    sS = np.ones((128, 4), bool)
    m2 = np.concatenate([band, s2], axis=1)
    mS = np.concatenate([band, sS], axis=1)
    full = np.concatenate([m0, m1, m2, mS], axis=1)
    masks = np.where(full, np.float32(0.0), np.float32(-1e30)).astype(ml_dtypes.bfloat16)
    idf = np.eye(128, dtype=np.float32)
    idb = np.eye(128).astype(ml_dtypes.bfloat16)
    return masks, idf, idb


def kernel(L0_attn_x, L0_attn_wq, L0_attn_wk, L0_attn_wv, L0_attn_wo, L0_ffn_weight, L0_ffn_bias):
    x = np.ascontiguousarray(np.asarray(L0_attn_x, dtype=np.float32))
    wq = np.asarray(L0_attn_wq, np.float32)
    wk = np.asarray(L0_attn_wk, np.float32)
    wv = np.asarray(L0_attn_wv, np.float32)
    wo = np.asarray(L0_attn_wo, np.float32)
    wf = np.ascontiguousarray(np.asarray(L0_ffn_weight, np.float32))
    bias = np.asarray(L0_ffn_bias, np.float32)

    wqT = np.ascontiguousarray(wq.T)
    wkT = np.ascontiguousarray(wk.T)
    wvT = np.ascontiguousarray(wv.T)
    woT = np.ascontiguousarray(wo.T)
    biasb = np.ascontiguousarray(np.broadcast_to(bias[None, :], (128, D)))
    masks, idf, idb = _build_consts()

    nc = build_nc()
    in_maps = []
    for i in range(NCORES):
        xi = np.ascontiguousarray(x[2 * i: 2 * i + 2].reshape(TOK, D))
        in_maps.append(dict(x=xi, wqT=wqT, wkT=wkT, wvT=wvT, woT=woT, wfO=wf,
                            biasb=biasb, idf=idf, idb=idb, masks=masks))
    res = run_bass_kernel_spmd(nc, in_maps, core_ids=list(range(NCORES)))
    out = np.empty((B, T, D), np.float32)
    for i in range(NCORES):
        out[2 * i: 2 * i + 2] = res.results[i]["y"].reshape(2, T, D)
    return out
